# version 07
# baseline (speedup 1.0000x reference)
import numpy as np
from contextlib import ExitStack

import concourse.bass as bass
import concourse.mybir as mybir
from concourse.bass_utils import run_bass_kernel_spmd

F32 = mybir.dt.float32
F16 = mybir.dt.float16
BF16 = mybir.dt.bfloat16
I32 = mybir.dt.int32
AF = mybir.ActivationFunctionType
ALU = mybir.AluOpType
AX = mybir.AxisListType

P = 128
T = 2048
D = 4096
NKC = D // P
H = 32
DH = 128
NIT = T // P
TOPK = 409
C_BIAS = 40.0
INV_SQRT_DH = float(1.0 / np.sqrt(DH))
ALPHA_MU = float(1.0 / (T * np.sqrt(DH)))
CM = float((0.8427 ** 2) / (DH * DH))
W_NEWTON = float(0.001747 / 0.8427)
EPS = 1e-5
N_NEWTON = 1

DEBUG = False

ENGS = ("tensor", "vector", "scalar", "gpsimd", "sync")


class Ctr:
    __slots__ = ("h", "n")

    def __init__(self, h):
        self.h = h
        self.n = 0


class Slot:
    __slots__ = ("ready", "users")

    def __init__(self, seed=None):
        self.ready = None
        self.users = list(seed) if seed else []

    def r(self):
        return [self.ready] if self.ready else []

    def w(self):
        ws = list(self.users)
        if self.ready:
            ws.append(self.ready)
        return ws

    def set_w(self, tok):
        self.ready = tok
        self.users = []

    def add_r(self, tok):
        self.users.append(tok)


class Buf:
    def __init__(self, t, slots=1, seed=None):
        self.t = t
        self.slot = [Slot(seed) for _ in range(slots)]


class KB:
    def __init__(self, nc, stack):
        self.nc = nc
        self.ops = {e: [] for e in ENGS}
        self.ctr = {e: Ctr(stack.enter_context(nc.semaphore(f"c_{e}"))) for e in ENGS}
        self.dctr = {e: Ctr(stack.enter_context(nc.semaphore(f"d_{e}")))
                     for e in ("sync", "scalar", "gpsimd")}
        self.dma_ctrs = set(self.dctr.values())
        self.gate = {}
        self.lastw = {e: {} for e in ENGS}

    def barrier_toks(self):
        out = [(c, c.n) for c in self.ctr.values() if c.n > 0]
        out += [(c, c.n) for c in self.dctr.values() if c.n > 0]
        return out

    def op(self, eng, build, waits=(), ctr=None, amt=1):
        ctr = ctr or self.ctr[eng]
        is_dma = ctr in self.dma_ctrs
        ws_in = list(waits)
        if is_dma and ctr in self.gate:
            ws_in.append(self.gate[ctr])
        ws = []
        seen = self.lastw[eng]
        best = {}
        waited_dma = []
        for (c, v) in ws_in:
            if v is None:
                continue
            if c in self.dma_ctrs:
                if is_dma and c is ctr:
                    continue
                v = c.n
                waited_dma.append(c)
            if v > best.get(c, 0):
                best[c] = v
        for c, v in best.items():
            if v > seen.get(c, 0):
                seen[c] = v
                ws.append((c, v))
        ctr.n += amt
        tok = (ctr, ctr.n)
        for c in waited_dma:
            self.gate[c] = tok
        self.ops[eng].append((ws, build, ctr.h, amt))
        return tok

    def dma(self, eng, out_ap, in_ap, waits=(), sctr=None, **kw):
        return self.op(eng, lambda E: E.dma_start(out=out_ap, in_=in_ap, **kw),
                       waits, ctr=(sctr or self.dctr[eng]), amt=16)

    def emit(self, blk):
        for e in ENGS:
            ops = self.ops[e]
            if not ops:
                continue

            def body(eng, ops=ops):
                for ws, build, sem, amt in ops:
                    for (c, v) in ws:
                        eng.wait_ge(c.h, v)
                    inst = build(eng)
                    if inst is not None:
                        inst.then_inc(sem, amt)

            getattr(blk, e)(body)


def build_nc():
    nc = bass.Bass("TRN2", target_bir_lowering=False)

    x_fm = nc.dram_tensor("x_fm", [D, T], F32, kind="ExternalInput").ap()
    wT = {k: nc.dram_tensor(f"w{k}T", [H * 8 * P, 4 * P], F32,
                            kind="ExternalInput").ap()
          for k in ("q", "k", "v", "o")}
    bias_d = {k: nc.dram_tensor(f"b{k}", [D], F32, kind="ExternalInput").ap()
              for k in ("q", "k", "v", "o")}
    qw_d = nc.dram_tensor("qw", [D // 8, D], I32, kind="ExternalInput").ap()
    qz_d = nc.dram_tensor("qz", [32, 512], I32, kind="ExternalInput").ap()
    sc_d = nc.dram_tensor("sc", [32, D], F32, kind="ExternalInput").ap()
    fb_d = nc.dram_tensor("fb", [D], F32, kind="ExternalInput").ap()

    out_fm = nc.dram_tensor("out_fm", [D, T], F32, kind="ExternalOutput").ap()

    ffn_scr = nc.dram_tensor("ffn_scr", [D, T], F32, kind="Internal").ap()
    ctx_scr = nc.dram_tensor("ctx_scr", [D, T], BF16, kind="Internal").ap()
    zs_dram = nc.dram_tensor("zs_dram", [32, D], F32, kind="Internal").ap()

    dbg = {}
    if DEBUG:
        for name, shape in [("dbg_xn", [D, T]), ("dbg_qT", [DH, T]),
                            ("dbg_kT", [DH, T]), ("dbg_v", [T, DH]),
                            ("dbg_s", [P, T]), ("dbg_t", [P, 16]),
                            ("dbg_p", [P, T]), ("dbg_ctx", [DH, T]),
                            ("dbg_ffn", [P, T]), ("dbg_zs", [32, D])]:
            dbg[name] = nc.dram_tensor(name, shape, F32, kind="ExternalOutput").ap()

    with ExitStack() as stack:
        b = KB(nc, stack)

        def sb(name, shape, dt, slots=1, st=None):
            t = (st or stack).enter_context(nc.sbuf_tensor(name, shape, dt))
            return Buf(t, slots, seed=b.barrier_toks())

        def dsem(name):
            return Ctr(stack.enter_context(nc.semaphore(name)))

        def ps(name, shape, dt, slots=1, st=None):
            t = (st or stack).enter_context(nc.psum_tensor(name, shape, dt))
            return Buf(t, slots, seed=b.barrier_toks())

        xn = sb("xn", [P, NKC, T], BF16, slots=NKC)
        ident = sb("ident", [P, P], BF16)
        iota_s = sb("iota_s", [P, P], I32)
        ones_col = sb("ones_col", [P, 1], BF16)
        ones_row = sb("ones_row", [1, P], BF16)
        cneg = sb("cneg", [P, 1], F32)
        eps_t = sb("eps_t", [1, 1], F32)
        bqh = sb("bqh", [P, H], F32)
        bkh = sb("bkh", [P, H], F32)
        bvh = sb("bvh", [P, H], F32)
        boh = sb("boh", [P, H], F32)
        fbh = sb("fbh", [P, H], F32)
        WRING = 8
        wring = {pr: [sb(f"wr_{pr}_{i}", [P, 4, P], BF16) for i in range(WRING)]
                 for pr in ("q", "k", "v")}
        wsem = {pr: [dsem(f"ws_{pr}_{i}") for i in range(WRING)]
                for pr in ("q", "k", "v")}

        ksum = sb("ksum", [P, 1], F32)
        ksum_bf = sb("ksum_bf", [P, 1], BF16)
        k2sum = sb("k2sum", [P, 1], F32)
        el_a = sb("el_a", [P, 1], F32)
        el_b = sb("el_b", [P, 1], F32)
        el_bf = sb("el_bf", [P, 1], BF16)
        kap_sb = sb("kap_sb", [1, 1], BF16)
        kapbc = sb("kapbc", [P, 1], F32)
        sgs = sb("sgs", [P, 16], F32, slots=16)
        term = sb("term", [P, 16], F32, slots=16)
        tcol = sb("tcol", [P, 16], F32, slots=16)
        wcol = sb("wcol", [P, 16], F32, slots=16)
        cnt = sb("cnt", [P, 16], F32, slots=16)
        tau = sb("tau", [P, 16], F32)
        tau2 = sb("tau2", [P, 16], F32, slots=16)
        biasall = sb("biasall", [P, 16], F32)
        zacc = sb("zacc", [P, 16], F32, slots=16)
        zinv = sb("zinv", [P, 16], F32, slots=16)
        diag = [sb(f"diag{i}", [P, P], BF16) for i in range(2)]

        def TS(eng, out, in0, s1, s2=None, op0=ALU.bypass, op1=None,
               accum=None, waits=()):
            def f(E, out=out, in0=in0, s1=s1, s2=s2, op0=op0, op1=op1, accum=accum):
                kw = dict(out=out, in0=in0, scalar1=s1, scalar2=s2, op0=op0)
                if op1 is not None:
                    kw["op1"] = op1
                if accum is not None:
                    kw["accum_out"] = accum
                return E.tensor_scalar(**kw)
            return b.op(eng, f, waits)

        def TT(eng, out, in0, in1, alu, waits=()):
            return b.op(eng, lambda E, out=out, in0=in0, in1=in1, alu=alu:
                        E.tensor_tensor(out=out, in0=in0, in1=in1, op=alu), waits)

        def STT(eng, out, in0, s, in1, op0, op1, accum=None, waits=()):
            def f(E, out=out, in0=in0, s=s, in1=in1, op0=op0, op1=op1, accum=accum):
                kw = dict(out=out, in0=in0, scalar=s, in1=in1, op0=op0, op1=op1)
                if accum is not None:
                    kw["accum_out"] = accum
                return E.scalar_tensor_tensor(**kw)
            return b.op(eng, f, waits)

        def ACT(out, in_, func, bias=None, scale=1.0, accum=None, waits=(), eng="scalar"):
            def f(E, out=out, in_=in_, func=func, bias=bias, scale=scale, accum=accum):
                kw = dict(out=out, in_=in_, func=func, scale=scale)
                if bias is not None:
                    kw["bias"] = bias
                if accum is not None:
                    kw["accum_out"] = accum
                return E.activation(**kw)
            return b.op(eng, f, waits)

        def MM(out, lhsT, rhs, start, stop, waits=()):
            return b.op("tensor",
                        lambda E, out=out, l=lhsT, r=rhs, st=start, sp=stop:
                        E.matmul(out, l, r, start=st, stop=sp), waits)

        def TRANS(out, in_, waits=()):
            return b.op("tensor",
                        lambda E, out=out, i=in_: E.transpose(out, i, ident.t[:]),
                        waits)

        def RED(eng, out, in_, alu, waits=()):
            return b.op(eng, lambda E, out=out, i=in_, a=alu:
                        E.tensor_reduce(out, i, axis=AX.X, op=a), waits)

        def RECIP(out, in_, waits=()):
            return b.op("vector", lambda E, o=out, i=in_: E.reciprocal(o, i), waits)

        tok = b.op("gpsimd", lambda E: E.iota(iota_s.t[:], pattern=[[-1, P]],
                                              base=0, channel_multiplier=1))
        iota_s.slot[0].set_w(tok)
        for buf, val in ((ones_col, 1.0), (ones_row, 1.0), (cneg, -C_BIAS),
                         (eps_t, EPS)):
            tok = b.op("gpsimd", lambda E, bu=buf, v=val: E.memset(bu.t[:], v),
                       waits=buf.slot[0].w())
            buf.slot[0].set_w(tok)
        tok = TS("vector", ident.t[:], iota_s.t[:], 0, op0=ALU.is_equal,
                 waits=iota_s.slot[0].r() + ident.slot[0].w())
        ident.slot[0].set_w(tok)
        iota_s.slot[0].add_r(tok)
        pbias = ExitStack()
        ident_f = sb("ident_f", [P, P], F32, st=pbias)
        tok = TS("vector", ident_f.t[:], iota_s.t[:], 0, op0=ALU.is_equal,
                 waits=iota_s.slot[0].r() + ident_f.slot[0].w())
        ident_f.slot[0].set_w(tok)
        iota_s.slot[0].add_r(tok)

        bstage = sb("bstage", [32, 5, P], F32, st=pbias)
        bias_ps = ps("bias_ps", [P, 5 * H], F32, st=pbias)
        bias_list = ((bias_d["q"], bqh), (bias_d["k"], bkh), (bias_d["v"], bvh),
                     (bias_d["o"], boh), (fb_d, fbh))
        for i, (dram, buf) in enumerate(bias_list):
            view = dram.rearrange("(h p) -> h p", p=P)
            tok = b.dma("sync", bstage.t[:, i, :], view, waits=bstage.slot[0].w())
            bstage.slot[0].set_w(tok)
        for i, (dram, buf) in enumerate(bias_list):
            w1 = bstage.slot[0].r() + ident_f.slot[0].r()
            if i == 0:
                w1 = w1 + bias_ps.slot[0].w()
            tok = b.op("tensor",
                       lambda E, i=i: E.transpose(bias_ps.t[:, i * H:(i + 1) * H],
                                                  bstage.t[:, i, :],
                                                  ident_f.t[0:32, 0:32]),
                       waits=w1)
            bstage.slot[0].add_r(tok)
            ident_f.slot[0].add_r(tok)
            if i == 4:
                bias_ps.slot[0].set_w(tok)
        for i, (dram, buf) in enumerate(bias_list):
            tok = TS("vector", buf.t[:], bias_ps.t[:, i * H:(i + 1) * H], 0.0,
                     op0=ALU.add, waits=bias_ps.slot[0].r() + buf.slot[0].w())
            buf.slot[0].set_w(tok)
            bias_ps.slot[0].add_r(tok)
        pbias.close()

        p0 = ExitStack()
        qz_sb = sb("qz_sb", [32, 512], I32, st=p0)
        sc_sb = sb("sc_sb", [32, D], F32, st=p0)
        zi32 = sb("zi32", [32, D], I32, st=p0)
        zs_sb = sb("zs_sb", [32, D], F32, st=p0)
        tok = b.dma("sync", qz_sb.t[:], qz_d, waits=qz_sb.slot[0].w())
        qz_sb.slot[0].set_w(tok)
        tok = b.dma("sync", sc_sb.t[:], sc_d, waits=sc_sb.slot[0].w())
        sc_sb.slot[0].set_w(tok)
        zview = zi32.t[:].rearrange("p (c e) -> p c e", e=8)
        for j in range(8):
            tok = TS("vector", zview[:, :, j], qz_sb.t[:], 4 * j, 15,
                     op0=ALU.logical_shift_right, op1=ALU.bitwise_and,
                     waits=qz_sb.slot[0].r() + zi32.slot[0].w())
            zi32.slot[0].set_w(tok)
            qz_sb.slot[0].add_r(tok)
        tok = TT("vector", zs_sb.t[:], zi32.t[:], sc_sb.t[:], ALU.mult,
                 waits=zi32.slot[0].r() + sc_sb.slot[0].r() + zs_sb.slot[0].w())
        zs_sb.slot[0].set_w(tok)
        zi32.slot[0].add_r(tok)
        sc_sb.slot[0].add_r(tok)
        zs_tok = b.dma("sync", zs_dram, zs_sb.t[:], waits=zs_sb.slot[0].r())
        zs_sb.slot[0].add_r(zs_tok)
        if DEBUG:
            tk = b.dma("gpsimd", dbg["dbg_zs"], zs_sb.t[:], waits=zs_sb.slot[0].r())
            zs_sb.slot[0].add_r(tk)
        p0.close()

        with ExitStack() as pa:
            xf32 = [sb(f"xf32_{i}", [P, T], F32, st=pa) for i in range(3)]
            xfsem = [dsem(f"xfs{i}") for i in range(3)]
            xfq = ("sync", "scalar", "gpsimd")
            x2b = [sb(f"x2b_{i}", [P, T], BF16, st=pa) for i in range(2)]
            mu_row = sb("mu_row", [1, 512], F32, st=pa)
            var_row = sb("var_row", [1, 512], F32, st=pa)
            std_row = sb("std_row", [1, 512], F32, st=pa)
            musq = var_row
            r_row = std_row
            mu_bfr = sb("mu_bfr", [1, T], BF16, st=pa)
            r_bfr = sb("r_bfr", [1, T], BF16, st=pa)

            with ExitStack() as pstats:
                sx_ps = [ps(f"sx_ps{i}", [1, 512], F32, st=pstats) for i in range(4)]
                sx2_ps = [ps(f"sx2_ps{i}", [1, 512], F32, st=pstats) for i in range(4)]

                for kc in range(NKC):
                    bf_ = kc % 3
                    bf2 = kc % 2
                    tok = b.dma(xfq[bf_], xf32[bf_].t[:], x_fm[kc * P:(kc + 1) * P, :],
                                waits=xf32[bf_].slot[0].w(), sctr=xfsem[bf_])
                    xf32[bf_].slot[0].set_w(tok)
                    tok = ACT(xn.t[:, kc, :], xf32[bf_].t[:], AF.Copy, bias=0.0,
                              waits=xf32[bf_].slot[0].r() + xn.slot[kc].w())
                    xn.slot[kc].set_w(tok)
                    xf32[bf_].slot[0].add_r(tok)
                    tok = STT("vector", x2b[bf2].t[:], xn.t[:, kc, :], 0.0,
                              xn.t[:, kc, :], ALU.bypass, ALU.mult,
                              waits=xn.slot[kc].r() + x2b[bf2].slot[0].w())
                    x2b[bf2].slot[0].set_w(tok)
                    xn.slot[kc].add_r(tok)
                    for n in range(4):
                        w1 = ones_col.slot[0].r() + xn.slot[kc].r()
                        if kc == 0:
                            w1 = w1 + sx_ps[n].slot[0].w()
                        tok = MM(sx_ps[n].t[:], ones_col.t[:],
                                 xn.t[:, kc, n * 512:(n + 1) * 512],
                                 start=(kc == 0), stop=(kc == NKC - 1), waits=w1)
                        xn.slot[kc].add_r(tok)
                        ones_col.slot[0].add_r(tok)
                        if kc == NKC - 1:
                            sx_ps[n].slot[0].set_w(tok)
                    for n in range(4):
                        w1 = ones_col.slot[0].r() + x2b[bf2].slot[0].r()
                        if kc == 0:
                            w1 = w1 + sx2_ps[n].slot[0].w()
                        tok = MM(sx2_ps[n].t[:], ones_col.t[:],
                                 x2b[bf2].t[:, n * 512:(n + 1) * 512],
                                 start=(kc == 0), stop=(kc == NKC - 1), waits=w1)
                        x2b[bf2].slot[0].add_r(tok)
                        ones_col.slot[0].add_r(tok)
                        if kc == NKC - 1:
                            sx2_ps[n].slot[0].set_w(tok)

                for n in range(4):
                    c0, c1 = n * 512, (n + 1) * 512
                    tok = TS("vector", mu_row.t[:], sx_ps[n].t[:], 1.0 / D,
                             op0=ALU.mult,
                             waits=sx_ps[n].slot[0].r() + mu_row.slot[0].w())
                    mu_row.slot[0].set_w(tok)
                    sx_ps[n].slot[0].add_r(tok)
                    tok = STT("vector", musq.t[:], mu_row.t[:], 0.0,
                              mu_row.t[:], ALU.bypass, ALU.mult,
                              waits=mu_row.slot[0].r() + musq.slot[0].w())
                    musq.slot[0].set_w(tok)
                    mu_row.slot[0].add_r(tok)
                    tok = TS("vector", mu_bfr.t[:, c0:c1], mu_row.t[:], 0.0,
                             op0=ALU.add,
                             waits=mu_row.slot[0].r() + mu_bfr.slot[0].w())
                    mu_bfr.slot[0].set_w(tok)
                    mu_row.slot[0].add_r(tok)
                    tok = STT("vector", var_row.t[:], sx2_ps[n].t[:], 1.0 / D,
                              musq.t[:], ALU.mult, ALU.subtract,
                              waits=sx2_ps[n].slot[0].r() + musq.slot[0].r()
                              + var_row.slot[0].w())
                    var_row.slot[0].set_w(tok)
                    sx2_ps[n].slot[0].add_r(tok)
                    musq.slot[0].add_r(tok)
                    tok = ACT(std_row.t[:], var_row.t[:], AF.Sqrt,
                              bias=eps_t.t[:],
                              waits=var_row.slot[0].r() + eps_t.slot[0].r()
                              + std_row.slot[0].w())
                    std_row.slot[0].set_w(tok)
                    var_row.slot[0].add_r(tok)
                    eps_t.slot[0].add_r(tok)
                    tok = RECIP(r_row.t[:], std_row.t[:],
                                waits=std_row.slot[0].r() + r_row.slot[0].w())
                    r_row.slot[0].set_w(tok)
                    std_row.slot[0].add_r(tok)
                    tok = TS("vector", r_bfr.t[:, c0:c1], r_row.t[:], 0.0,
                             op0=ALU.add,
                             waits=r_row.slot[0].r() + r_bfr.slot[0].w())
                    r_bfr.slot[0].set_w(tok)
                    r_row.slot[0].add_r(tok)

            with ExitStack() as pbc:
                mu_bc = ps("mu_bc", [P, T], F32, st=pbc)
                r_bc = ps("r_bc", [P, T], F32, st=pbc)
                for n in range(4):
                    c0, c1 = n * 512, (n + 1) * 512
                    tok = MM(mu_bc.t[:, c0:c1], ones_row.t[:], mu_bfr.t[:, c0:c1],
                             start=True, stop=True,
                             waits=ones_row.slot[0].r() + mu_bfr.slot[0].r()
                             + mu_bc.slot[0].w())
                    mu_bc.slot[0].set_w(tok)
                    mu_bfr.slot[0].add_r(tok)
                    ones_row.slot[0].add_r(tok)
                    tok = MM(r_bc.t[:, c0:c1], ones_row.t[:], r_bfr.t[:, c0:c1],
                             start=True, stop=True,
                             waits=ones_row.slot[0].r() + r_bfr.slot[0].r()
                             + r_bc.slot[0].w())
                    r_bc.slot[0].set_w(tok)
                    r_bfr.slot[0].add_r(tok)
                    ones_row.slot[0].add_r(tok)
                for kc in range(NKC):
                    bf_ = kc % 3
                    scr = xf32[bf_]
                    tok = TT("vector", scr.t[:], xn.t[:, kc, :], mu_bc.t[:],
                             ALU.subtract,
                             waits=xn.slot[kc].r() + mu_bc.slot[0].r()
                             + scr.slot[0].w())
                    scr.slot[0].set_w(tok)
                    xn.slot[kc].add_r(tok)
                    mu_bc.slot[0].add_r(tok)
                    tok = TT("vector", xn.t[:, kc, :], scr.t[:], r_bc.t[:], ALU.mult,
                             waits=scr.slot[0].r() + r_bc.slot[0].r()
                             + xn.slot[kc].w())
                    xn.slot[kc].set_w(tok)
                    scr.slot[0].add_r(tok)
                    r_bc.slot[0].add_r(tok)
                    if DEBUG:
                        tk = b.dma("gpsimd", dbg["dbg_xn"][kc * P:(kc + 1) * P, :],
                                   xn.t[:, kc, :], waits=xn.slot[kc].r())
                        xn.slot[kc].add_r(tk)

        WQUEUE = {"q": "gpsimd", "k": "gpsimd", "v": "gpsimd"}

        def prefetch_head(h):
            toks = {}
            for pr in ("q", "k", "v"):
                lst = []
                for g in range(8):
                    sl = wring[pr][g]
                    base = (h * 8 + g) * P
                    view = wT[pr][base:base + P, :].rearrange(
                        "p (j c) -> p j c", c=P)
                    tok = b.dma(WQUEUE[pr], sl.t[:], view,
                                waits=sl.slot[0].w(), sctr=wsem[pr][g])
                    sl.slot[0].set_w(tok)
                    lst.append(tok)
                toks[pr] = lst
            return toks

        wtoks = prefetch_head(0)

        ffn_out_toks = [[] for _ in range(NKC)]
        with ExitStack() as pb:
            ffn_ps = [ps(f"ffn_ps{i}", [P, T], F32, st=pb) for i in range(2)]
            W_sb = [sb(f"W_sb{i}", [P, NKC, 256], BF16, slots=NKC, st=pb)
                    for i in range(2)]
            qwr = [sb(f"qwr{i}", [P, 256], I32, st=pb) for i in range(2)]
            zbc = [sb(f"zbc{i}", [P, 256], F32, st=pb) for i in range(3)]
            sbc = [sb(f"sbc{i}", [P, 256], F32, st=pb) for i in range(3)]
            qwsem = [dsem(f"qws{i}") for i in range(2)]
            zbsem = [dsem(f"zbs{i}") for i in range(3)]
            sbsem = [dsem(f"sbs{i}") for i in range(3)]
            wi32 = sb("wi32", [P, 256], I32, st=pb)
            shifts = sb("shifts", [P, 256], I32, st=pb)
            tok = b.op("gpsimd", lambda E: E.iota(shifts.t[:], pattern=[[0, 256]],
                                                  base=0, channel_multiplier=4),
                       waits=shifts.slot[0].w())
            shifts.slot[0].set_w(tok)
            tok = TS("vector", shifts.t[:], shifts.t[:], 31, op0=ALU.bitwise_and,
                     waits=shifts.slot[0].r())
            shifts.slot[0].set_w(tok)
            wtmp = sb("wtmp", [P, 256], F32, st=pb)
            fstage = [sb(f"fstage{i}", [P, 1024], F32, st=pb) for i in range(2)]

            def dequant_half(b2, dst, kcs):
                co = b2 * 256
                for kc in kcs:
                    sl = kc % 3
                    sq = kc % 2
                    src = qw_d[kc * 16:(kc + 1) * 16, co:co + 256]
                    src = src.rearrange("r (u c) -> r u c", u=1).broadcast_to([16, 8, 256])
                    tok = b.dma("sync", qwr[sq].t[:], src,
                                waits=qwr[sq].slot[0].w(), sctr=qwsem[sq])
                    qwr[sq].slot[0].set_w(tok)
                    zsrc = zs_dram[kc:kc + 1, co:co + 256]
                    zsrc = zsrc.rearrange("r (u c) -> r u c", u=1).broadcast_to([1, P, 256])
                    tok = b.dma("scalar", zbc[sl].t[:], zsrc,
                                waits=zbc[sl].slot[0].w() + [zs_tok], sctr=zbsem[sl])
                    zbc[sl].slot[0].set_w(tok)
                    ssrc = sc_d[kc:kc + 1, co:co + 256]
                    ssrc = ssrc.rearrange("r (u c) -> r u c", u=1).broadcast_to([1, P, 256])
                    tok = b.dma("scalar", sbc[sl].t[:], ssrc,
                                waits=sbc[sl].slot[0].w(), sctr=sbsem[sl])
                    sbc[sl].slot[0].set_w(tok)

                    tok = TT("vector", wi32.t[:], qwr[sq].t[:], shifts.t[:],
                             ALU.logical_shift_right,
                             waits=qwr[sq].slot[0].r() + shifts.slot[0].r()
                             + wi32.slot[0].w())
                    wi32.slot[0].set_w(tok)
                    qwr[sq].slot[0].add_r(tok)
                    shifts.slot[0].add_r(tok)
                    tok = TS("vector", wi32.t[:], wi32.t[:], 15, op0=ALU.bitwise_and,
                             waits=wi32.slot[0].r())
                    wi32.slot[0].set_w(tok)
                    tok = TT("vector", wtmp.t[:], wi32.t[:], sbc[sl].t[:], ALU.mult,
                             waits=wi32.slot[0].r() + sbc[sl].slot[0].r()
                             + wtmp.slot[0].w())
                    wtmp.slot[0].set_w(tok)
                    wi32.slot[0].add_r(tok)
                    sbc[sl].slot[0].add_r(tok)
                    tok = TT("vector", dst.t[:, kc, :], wtmp.t[:], zbc[sl].t[:],
                             ALU.subtract,
                             waits=wtmp.slot[0].r() + zbc[sl].slot[0].r()
                             + dst.slot[kc].w())
                    dst.slot[kc].set_w(tok)
                    wtmp.slot[0].add_r(tok)
                    zbc[sl].slot[0].add_r(tok)

            dequant_half(0, W_sb[0], range(NKC))
            for b2 in range(16):
                cur = W_sb[b2 % 2]
                for ot_l in range(2):
                    ot = 2 * b2 + ot_l
                    fps = ffn_ps[ot % 2]
                    for kc in range(NKC):
                        lhs = cur.t[:, kc, ot_l * P:(ot_l + 1) * P]
                        for n in range(4):
                            w1 = cur.slot[kc].r() + xn.slot[kc].r()
                            if kc == 0 and n == 0:
                                w1 = w1 + fps.slot[0].w()
                            tok = MM(fps.t[:, n * 512:(n + 1) * 512], lhs,
                                     xn.t[:, kc, n * 512:(n + 1) * 512],
                                     start=(kc == 0), stop=(kc == NKC - 1), waits=w1)
                            cur.slot[kc].add_r(tok)
                            xn.slot[kc].add_r(tok)
                            if kc == NKC - 1 and n == 3:
                                fps.slot[0].set_w(tok)
                    if b2 + 1 < 16:
                        dequant_half(b2 + 1, W_sb[(b2 + 1) % 2],
                                     range(ot_l * 16, (ot_l + 1) * 16))
                    for hf in range(2):
                        st_ = fstage[hf]
                        c0, c1 = hf * 1024, (hf + 1) * 1024
                        tok = ACT(st_.t[:], fps.t[:, c0:c1], AF.Copy, bias=0.0,
                                  waits=fps.slot[0].r() + st_.slot[0].w())
                        st_.slot[0].set_w(tok)
                        fps.slot[0].add_r(tok)
                        tok = TS("gpsimd", st_.t[:], st_.t[:],
                                 fbh.t[:, ot:ot + 1], op0=ALU.add,
                                 waits=st_.slot[0].r() + fbh.slot[0].r())
                        st_.slot[0].set_w(tok)
                        fbh.slot[0].add_r(tok)
                        tok = b.dma("sync", ffn_scr[ot * P:(ot + 1) * P, c0:c1],
                                    st_.t[:], waits=st_.slot[0].r())
                        st_.slot[0].add_r(tok)
                        ffn_out_toks[ot].append(tok)
                        if DEBUG and ot == 0:
                            tk = b.dma("gpsimd", dbg["dbg_ffn"][:, c0:c1], st_.t[:],
                                       waits=st_.slot[0].r())
                            st_.slot[0].add_r(tk)

        ctx_dma_toks = [None] * H
        ctxsem = [dsem(f"ctxs{i}") for i in range(2)]
        with ExitStack() as pc:
            s_psA = ps("s_psA", [P, 1024], F32, st=pc)
            s_psB = ps("s_psB", [P, 1024], F32, st=pc)
            tr_ps = [ps(f"tr_ps{i}", [P, 512], F32, st=pc) for i in range(2)]
            ctx_ps = ps("ctx_ps", [P, 512], F32, st=pc)
            st_ps = ps("st_ps", [P, 512], F32, st=pc)

            qT = sb("qT", [P, T], BF16, st=pc)
            kT = sb("kT", [P, T], BF16, st=pc)
            vT = sb("vT", [P, T], BF16, st=pc)
            v_sb = sb("v_sb", [P, T], BF16, slots=NIT, st=pc)
            e_sb = [sb(f"e_sb{i}", [P, T], BF16, st=pc) for i in range(2)]
            p_sb = [sb(f"p_sb{i}", [P, T], BF16, st=pc) for i in range(2)]
            pT = sb("pT", [P, 16 * 512], BF16, slots=16, st=pc)
            ctx_sb = sb("ctx_sb", [P, T], BF16, slots=4, st=pc)

            pT_v = pT.t[:].rearrange("p (jc i) -> p jc i", i=512)
            tr_v = [tr_ps[i].t[:].rearrange("p (j c) -> p j c", c=P)
                    for i in range(2)]
            v_sb_v = v_sb.t[:].rearrange("p (jc c) -> p jc c", c=P)
            st_v = st_ps.t[:].rearrange("p (c e) -> p c e", e=2)

            for h in range(H):
                def proj(pr, evac_dst, bias_buf, dst_slot):
                    for kc in range(NKC):
                        sl = wring[pr][kc // 4]
                        lhs = sl.t[:, kc % 4, :]
                        for n in range(4):
                            pst = s_psA if n < 2 else s_psB
                            col = (n % 2) * 512
                            w1 = [wtoks[pr][kc // 4]] + xn.slot[kc].r()
                            if kc == 0 and n % 2 == 0:
                                w1 = w1 + pst.slot[0].w()
                            tok = MM(pst.t[:, col:col + 512], lhs,
                                     xn.t[:, kc, n * 512:(n + 1) * 512],
                                     start=(kc == 0), stop=(kc == NKC - 1), waits=w1)
                            sl.slot[0].add_r(tok)
                            xn.slot[kc].add_r(tok)
                            if kc == NKC - 1:
                                pst.slot[0].set_w(tok)
                    for half, pst in ((0, s_psA), (1, s_psB)):
                        tok = TS("vector", evac_dst[:, half * 1024:(half + 1) * 1024],
                                 pst.t[:], bias_buf.t[:, h:h + 1], op0=ALU.add,
                                 waits=pst.slot[0].r() + bias_buf.slot[0].r()
                                 + dst_slot.w())
                        dst_slot.set_w(tok)
                        pst.slot[0].add_r(tok)
                        bias_buf.slot[0].add_r(tok)

                q2 = vT
                proj("q", qT.t[:], bqh, qT.slot[0])
                proj("k", kT.t[:], bkh, kT.slot[0])
                proj("v", vT.t[:], bvh, vT.slot[0])

                if h + 1 < H:
                    wtoks = prefetch_head(h + 1)
                if h == H - 1:
                    for hc in range(NKC - 1):
                        qn = ("scalar", "sync")[hc % 2]
                        tok = b.dma(qn, xn.t[:, hc, :],
                                    ctx_scr[hc * P:(hc + 1) * P, :],
                                    waits=xn.slot[hc].w() + [ctx_dma_toks[hc]],
                                    sctr=ctxsem[hc % 2])
                        xn.slot[hc].set_w(tok)

                for rnd in range(4):
                    pi = rnd % 2
                    tp = tr_ps[pi]
                    tvv = tr_v[pi]
                    for j4 in range(4):
                        jc = rnd * 4 + j4
                        w1 = vT.slot[0].r() + ident.slot[0].r()
                        if j4 == 0:
                            w1 = w1 + tp.slot[0].w()
                        tok = MM(tvv[:, j4, :], vT.t[:, jc * P:(jc + 1) * P],
                                 ident.t[:], start=True, stop=True, waits=w1)
                        if j4 == 3:
                            tp.slot[0].set_w(tok)
                        vT.slot[0].add_r(tok)
                        ident.slot[0].add_r(tok)
                    eng = "vector" if pi == 0 else "scalar"
                    dst = v_sb_v[:, rnd * 4:(rnd + 1) * 4, :]
                    wsl = []
                    for jj in range(4):
                        wsl += v_sb.slot[rnd * 4 + jj].w()
                    if eng == "vector":
                        tok = TS("vector", dst, tvv[:, :, :], 0.0, op0=ALU.add,
                                 waits=tp.slot[0].r() + wsl)
                    else:
                        tok = ACT(dst, tvv[:, :, :], AF.Copy, bias=0.0,
                                  waits=tp.slot[0].r() + wsl)
                    for jj in range(4):
                        v_sb.slot[rnd * 4 + jj].set_w(tok)
                    tp.slot[0].add_r(tok)

                if DEBUG and h == 1:
                    for nm, bb in (("dbg_qT", qT), ("dbg_kT", kT)):
                        tk = b.dma("gpsimd", dbg[nm], bb.t[:], waits=bb.slot[0].r())
                        bb.slot[0].add_r(tk)

                tok = RED("vector", ksum.t[:], kT.t[:], ALU.add,
                          waits=kT.slot[0].r() + ksum.slot[0].w())
                ksum.slot[0].set_w(tok)
                kT.slot[0].add_r(tok)
                tok = TS("vector", ksum_bf.t[:], ksum.t[:], 0.0, op0=ALU.add,
                         waits=ksum.slot[0].r() + ksum_bf.slot[0].w())
                ksum_bf.slot[0].set_w(tok)
                ksum.slot[0].add_r(tok)
                tok = STT("vector", p_sb[0].t[:], kT.t[:], 0.0, kT.t[:],
                          ALU.bypass, ALU.mult, accum=k2sum.t[:],
                          waits=kT.slot[0].r() + p_sb[0].slot[0].w()
                          + k2sum.slot[0].w())
                p_sb[0].slot[0].set_w(tok)
                k2sum.slot[0].set_w(tok)
                kT.slot[0].add_r(tok)
                tok = STT("vector", q2.t[:], qT.t[:], 0.0, qT.t[:],
                          ALU.bypass, ALU.mult,
                          waits=qT.slot[0].r() + q2.slot[0].w())
                q2.slot[0].set_w(tok)
                qT.slot[0].add_r(tok)
                tok = TS("vector", el_a.t[:], k2sum.t[:], CM / T, op0=ALU.mult,
                         waits=k2sum.slot[0].r() + el_a.slot[0].w())
                el_a.slot[0].set_w(tok)
                k2sum.slot[0].add_r(tok)
                tok = STT("vector", el_b.t[:], ksum.t[:], CM / (T * T), ksum.t[:],
                          ALU.mult, ALU.mult,
                          waits=ksum.slot[0].r() + el_b.slot[0].w())
                el_b.slot[0].set_w(tok)
                ksum.slot[0].add_r(tok)
                tok = TT("vector", el_a.t[:], el_a.t[:], el_b.t[:], ALU.subtract,
                         waits=el_a.slot[0].r() + el_b.slot[0].r())
                el_a.slot[0].set_w(tok)
                el_b.slot[0].add_r(tok)
                tok = TS("vector", el_bf.t[:], el_a.t[:], 0.0, op0=ALU.add,
                         waits=el_a.slot[0].r() + el_bf.slot[0].w())
                el_bf.slot[0].set_w(tok)
                el_a.slot[0].add_r(tok)
                tok = MM(st_ps.t[0:1, 200:201], el_bf.t[:], ones_col.t[:],
                         start=True, stop=True,
                         waits=el_bf.slot[0].r() + ones_col.slot[0].r()
                         + st_ps.slot[0].w())
                st_ps.slot[0].set_w(tok)
                el_bf.slot[0].add_r(tok)
                ones_col.slot[0].add_r(tok)
                tok = TS("vector", kap_sb.t[:], st_ps.t[0:1, 200:201], 0.0,
                         op0=ALU.add,
                         waits=st_ps.slot[0].r() + kap_sb.slot[0].w())
                kap_sb.slot[0].set_w(tok)
                st_ps.slot[0].add_r(tok)
                tok = MM(st_ps.t[:, 210:211], ones_row.t[:], kap_sb.t[:],
                         start=True, stop=True,
                         waits=ones_row.slot[0].r() + kap_sb.slot[0].r()
                         + st_ps.slot[0].w())
                st_ps.slot[0].set_w(tok)
                kap_sb.slot[0].add_r(tok)
                ones_row.slot[0].add_r(tok)
                tok = TS("vector", kapbc.t[:], st_ps.t[:, 210:211], 0.0, op0=ALU.add,
                         waits=st_ps.slot[0].r() + kapbc.slot[0].w())
                kapbc.slot[0].set_w(tok)
                st_ps.slot[0].add_r(tok)

                for it in range(NIT):
                    tok = MM(st_ps.t[:, 2 * it:2 * it + 1],
                             qT.t[:, it * P:(it + 1) * P], ksum_bf.t[:],
                             start=True, stop=True,
                             waits=qT.slot[0].r() + ksum_bf.slot[0].r()
                             + st_ps.slot[0].w())
                    st_ps.slot[0].set_w(tok)
                    qT.slot[0].add_r(tok)
                    ksum_bf.slot[0].add_r(tok)
                    tok = MM(st_ps.t[:, 2 * it + 1:2 * it + 2],
                             q2.t[:, it * P:(it + 1) * P], ones_col.t[:],
                             start=True, stop=True,
                             waits=q2.slot[0].r() + ones_col.slot[0].r()
                             + st_ps.slot[0].w())
                    st_ps.slot[0].set_w(tok)
                    q2.slot[0].add_r(tok)
                    ones_col.slot[0].add_r(tok)

                tok = TS("vector", sgs.t[:], st_v[:, 0:16, 1], kapbc.t[:],
                         op0=ALU.mult,
                         waits=st_ps.slot[0].r() + kapbc.slot[0].r()
                         + sgs.slot[0].w())
                for ii in range(16):
                    sgs.slot[ii].set_w(tok)
                st_ps.slot[0].add_r(tok)
                kapbc.slot[0].add_r(tok)
                tok = ACT(term.t[:], sgs.t[:], AF.Sqrt, bias=0.0,
                          waits=sgs.slot[0].r() + term.slot[0].w())
                for ii in range(16):
                    term.slot[ii].set_w(tok)
                sgs.slot[0].add_r(tok)
                tok = STT("vector", tcol.t[:], st_v[:, 0:16, 0], ALPHA_MU,
                          term.t[:], ALU.mult, ALU.add,
                          waits=st_ps.slot[0].r() + term.slot[0].r()
                          + sum((tcol.slot[ii].w() for ii in range(16)), []))
                for ii in range(16):
                    tcol.slot[ii].set_w(tok)
                st_ps.slot[0].add_r(tok)
                term.slot[0].add_r(tok)
                tok = TS("vector", tcol.t[:], tcol.t[:], -C_BIAS, op0=ALU.add,
                         waits=tcol.slot[0].r())
                for ii in range(16):
                    tcol.slot[ii].set_w(tok)
                tok = TS("vector", wcol.t[:], term.t[:], -W_NEWTON / 2.0,
                         op0=ALU.mult,
                         waits=term.slot[0].r()
                         + sum((wcol.slot[ii].w() for ii in range(16)), []))
                for ii in range(16):
                    wcol.slot[ii].set_w(tok)
                term.slot[0].add_r(tok)
                tok = STT("vector", biasall.t[:], wcol.t[:], float(2 * TOPK - T),
                          tcol.t[:], ALU.mult, ALU.add,
                          waits=wcol.slot[0].r() + tcol.slot[0].r()
                          + biasall.slot[0].w())
                biasall.slot[0].set_w(tok)
                wcol.slot[0].add_r(tok)
                tcol.slot[0].add_r(tok)
                tok = ACT(tau.t[:], tcol.t[:], AF.Exp, bias=0.0,
                          waits=tcol.slot[0].r() + tau.slot[0].w())
                tau.slot[0].set_w(tok)
                for ii in range(16):
                    tcol.slot[ii].add_r(tok)

                def part1(it):
                    pb_ = p_sb[it % 2]
                    eb_ = e_sb[it % 2]
                    ic = it
                    for half, pst in ((0, s_psA), (1, s_psB)):
                        for c in range(2):
                            j0 = half * 1024 + c * 512
                            w1 = qT.slot[0].r() + kT.slot[0].r()
                            if c == 0:
                                w1 = w1 + pst.slot[0].w()
                            tok = MM(pst.t[:, c * 512:(c + 1) * 512],
                                     qT.t[:, it * P:(it + 1) * P],
                                     kT.t[:, j0:j0 + 512],
                                     start=True, stop=True, waits=w1)
                            qT.slot[0].add_r(tok)
                            kT.slot[0].add_r(tok)
                            if c == 1:
                                pst.slot[0].set_w(tok)
                    for half, pst in ((0, s_psA), (1, s_psB)):
                        tok = ACT(eb_.t[:, half * 1024:(half + 1) * 1024],
                                  pst.t[:], AF.Exp, bias=cneg.t[:],
                                  scale=INV_SQRT_DH,
                                  waits=pst.slot[0].r() + cneg.slot[0].r()
                                  + eb_.slot[0].w())
                        eb_.slot[0].set_w(tok)
                        pst.slot[0].add_r(tok)
                        cneg.slot[0].add_r(tok)
                    tok = ACT(q2.t[:], eb_.t[:], AF.Sign, bias=tau.t[:, ic:ic + 1],
                              scale=-1.0, accum=cnt.t[:, ic:ic + 1],
                              waits=eb_.slot[0].r() + tau.slot[0].r()
                              + q2.slot[0].w() + cnt.slot[ic].w())
                    q2.slot[0].set_w(tok)
                    cnt.slot[ic].set_w(tok)
                    eb_.slot[0].add_r(tok)
                    tau.slot[0].add_r(tok)
                    tok = ACT(tau2.t[:, ic:ic + 1], cnt.t[:, ic:ic + 1], AF.Exp,
                              scale=wcol.t[:, ic:ic + 1],
                              bias=biasall.t[:, ic:ic + 1],
                              waits=cnt.slot[ic].r() + wcol.slot[ic].r()
                              + biasall.slot[0].r() + tau2.slot[ic].w())
                    tau2.slot[ic].set_w(tok)
                    cnt.slot[ic].add_r(tok)
                    wcol.slot[ic].add_r(tok)
                    biasall.slot[0].add_r(tok)
                    tok = STT("vector", pb_.t[:], eb_.t[:], tau2.t[:, ic:ic + 1],
                              eb_.t[:], ALU.is_ge, ALU.mult,
                              accum=zacc.t[:, ic:ic + 1],
                              waits=eb_.slot[0].r() + tau2.slot[ic].r()
                              + pb_.slot[0].w() + zacc.slot[ic].w())
                    pb_.slot[0].set_w(tok)
                    zacc.slot[ic].set_w(tok)
                    eb_.slot[0].add_r(tok)
                    tau2.slot[ic].add_r(tok)
                    tok = RECIP(zinv.t[:, ic:ic + 1], zacc.t[:, ic:ic + 1],
                                waits=zacc.slot[ic].r() + zinv.slot[ic].w())
                    zinv.slot[ic].set_w(tok)
                    zacc.slot[ic].add_r(tok)
                    dg = diag[it % 2]
                    tok = TS("vector", dg.t[:], ident.t[:], zinv.t[:, ic:ic + 1],
                             op0=ALU.mult,
                             waits=ident.slot[0].r() + zinv.slot[ic].r()
                             + dg.slot[0].w())
                    dg.slot[0].set_w(tok)
                    ident.slot[0].add_r(tok)
                    zinv.slot[ic].add_r(tok)
                    if DEBUG and h == 1 and it == 0:
                        tk = b.dma("gpsimd", dbg["dbg_s"], eb_.t[:],
                                   waits=eb_.slot[0].r())
                        eb_.slot[0].add_r(tk)
                        tk = b.dma("gpsimd", dbg["dbg_t"], tcol.t[:],
                                   waits=tcol.slot[ic].r())
                        tcol.slot[ic].add_r(tk)
                        tk = b.dma("gpsimd", dbg["dbg_p"], pb_.t[:],
                                   waits=pb_.slot[0].r())
                        pb_.slot[0].add_r(tk)

                def part2(it):
                    pb_ = p_sb[it % 2]
                    wpos = it % 4
                    dg = diag[it % 2]
                    for rnd in range(4):
                        pi = (it * 4 + rnd) % 2
                        tp = tr_ps[pi]
                        tvv = tr_v[pi]
                        for j4 in range(4):
                            jc = rnd * 4 + j4
                            w1 = pb_.slot[0].r() + dg.slot[0].r()
                            if j4 == 0:
                                w1 = w1 + tp.slot[0].w()
                            tok = MM(tvv[:, j4, :], pb_.t[:, jc * P:(jc + 1) * P],
                                     dg.t[:], start=True, stop=True, waits=w1)
                            if j4 == 3:
                                tp.slot[0].set_w(tok)
                            pb_.slot[0].add_r(tok)
                            dg.slot[0].add_r(tok)
                        sidx = rnd * 4 + wpos
                        dstv = pT_v[:, rnd * 4:(rnd + 1) * 4,
                                    wpos * P:(wpos + 1) * P]
                        if rnd % 2 == 0:
                            tok = TS("vector", dstv, tvv[:, :, :], 0.0, op0=ALU.add,
                                     waits=tp.slot[0].r() + pT.slot[sidx].w())
                        else:
                            tok = ACT(dstv, tvv[:, :, :], AF.Copy, bias=0.0,
                                      waits=tp.slot[0].r() + pT.slot[sidx].w())
                        pT.slot[sidx].set_w(tok)
                        tp.slot[0].add_r(tok)

                    if wpos == 3:
                        wnd = it // 4
                        for jc in range(16):
                            w1 = v_sb.slot[jc].r()
                            rg = jc // 4
                            for j in range(4):
                                w1 = w1 + pT.slot[rg * 4 + j].r()
                            if jc == 0:
                                w1 = w1 + ctx_ps.slot[0].w()
                            tok = MM(ctx_ps.t[:], v_sb.t[:, jc * P:(jc + 1) * P],
                                     pT.t[:, jc * 512:(jc + 1) * 512],
                                     start=(jc == 0), stop=(jc == 15), waits=w1)
                            v_sb.slot[jc].add_r(tok)
                            for s2 in range(16):
                                pT.slot[s2].add_r(tok)
                            if jc == 15:
                                ctx_ps.slot[0].set_w(tok)
                        tok = TS("vector", ctx_sb.t[:, wnd * 512:(wnd + 1) * 512],
                                 ctx_ps.t[:], 0.0, op0=ALU.add,
                                 waits=ctx_ps.slot[0].r() + ctx_sb.slot[wnd].w())
                        ctx_sb.slot[wnd].set_w(tok)
                        ctx_ps.slot[0].add_r(tok)

                part1(0)
                if NIT > 1:
                    part1(1)
                for it in range(NIT):
                    part2(it)
                    if it + 2 < NIT:
                        part1(it + 2)

                wv = []
                for wnd in range(4):
                    wv += ctx_sb.slot[wnd].r()
                tok = b.dma("sync", ctx_scr[h * P:(h + 1) * P, :], ctx_sb.t[:],
                            waits=wv)
                for wnd in range(4):
                    ctx_sb.slot[wnd].add_r(tok)
                ctx_dma_toks[h] = tok
                if DEBUG and h == 1:
                    tk = b.dma("gpsimd", dbg["dbg_ctx"], ctx_sb.t[:], waits=wv)
                    for wnd in range(4):
                        ctx_sb.slot[wnd].add_r(tk)

        out_toks = []
        with ExitStack() as pd:
            out_ps = [ps(f"out_ps{i}", [P, T], F32, st=pd) for i in range(2)]
            ffn_t = [sb(f"ffn_t{i}", [P, T], F32, st=pd) for i in range(2)]
            x_t = [sb(f"x_t{i}", [P, T], F32, st=pd) for i in range(2)]
            ftsem = [dsem(f"fts{i}") for i in range(2)]
            xtsem = [dsem(f"xts{i}") for i in range(2)]
            stage = [sb(f"stage{i}", [P, T], F32, st=pd) for i in range(2)]

            tok = b.dma("scalar", xn.t[:, NKC - 1, :],
                        ctx_scr[(NKC - 1) * P:NKC * P, :],
                        waits=xn.slot[NKC - 1].w() + [ctx_dma_toks[NKC - 1]],
                        sctr=ctxsem[0])
            xn.slot[NKC - 1].set_w(tok)

            wo_ring = wring["q"] + wring["k"] + wring["v"]
            NR = len(wo_ring)
            wo_sems = wsem["q"] + wsem["k"] + wsem["v"]
            wo_toks = {}

            def prefetch_ot(ot):
                for g in range(8):
                    idx = ot * 8 + g
                    sl = wo_ring[idx % NR]
                    base = (ot * 8 + g) * P
                    view = wT["o"][base:base + P, :].rearrange(
                        "p (j c) -> p j c", c=P)
                    tok = b.dma("gpsimd", sl.t[:], view, waits=sl.slot[0].w(),
                                sctr=wo_sems[idx % NR])
                    sl.slot[0].set_w(tok)
                    wo_toks[idx] = tok

            prefetch_ot(0)
            prefetch_ot(1)
            tok = b.dma("scalar", ffn_t[0].t[:], ffn_scr[0:P, :],
                        waits=ffn_t[0].slot[0].w() + ffn_out_toks[0],
                        sctr=ftsem[0])
            ffn_t[0].slot[0].set_w(tok)
            tok = b.dma("scalar", x_t[0].t[:], x_fm[0:P, :],
                        waits=x_t[0].slot[0].w(), sctr=xtsem[0])
            x_t[0].slot[0].set_w(tok)
            for ot in range(NKC):
                if ot + 2 < NKC:
                    prefetch_ot(ot + 2)
                if ot + 1 < NKC:
                    ftn = ffn_t[(ot + 1) % 2]
                    tok = b.dma("scalar", ftn.t[:],
                                ffn_scr[(ot + 1) * P:(ot + 2) * P, :],
                                waits=ftn.slot[0].w() + ffn_out_toks[ot + 1],
                                sctr=ftsem[(ot + 1) % 2])
                    ftn.slot[0].set_w(tok)
                    xtn = x_t[(ot + 1) % 2]
                    tok = b.dma("scalar", xtn.t[:],
                                x_fm[(ot + 1) * P:(ot + 2) * P, :],
                                waits=xtn.slot[0].w(), sctr=xtsem[(ot + 1) % 2])
                    xtn.slot[0].set_w(tok)

                ops_ = out_ps[ot % 2]
                for hc in range(NKC):
                    idx = ot * 8 + hc // 4
                    sl = wo_ring[idx % NR]
                    lhs = sl.t[:, hc % 4, :]
                    for n in range(4):
                        w1 = [wo_toks[idx]] + xn.slot[hc].r()
                        if hc == 0 and n == 0:
                            w1 = w1 + ops_.slot[0].w()
                        tok = MM(ops_.t[:, n * 512:(n + 1) * 512], lhs,
                                 xn.t[:, hc, n * 512:(n + 1) * 512],
                                 start=(hc == 0), stop=(hc == NKC - 1), waits=w1)
                        sl.slot[0].add_r(tok)
                        xn.slot[hc].add_r(tok)
                        if hc == NKC - 1 and n == 3:
                            ops_.slot[0].set_w(tok)

                st_ = stage[ot % 2]
                ft = ffn_t[ot % 2]
                for hf in range(2):
                    c0, c1 = hf * 1024, (hf + 1) * 1024
                    tok = TS("vector", st_.t[:, c0:c1], ops_.t[:, c0:c1],
                             boh.t[:, ot:ot + 1], op0=ALU.add,
                             waits=ops_.slot[0].r() + boh.slot[0].r()
                             + st_.slot[0].w())
                    st_.slot[0].set_w(tok)
                    ops_.slot[0].add_r(tok)
                    boh.slot[0].add_r(tok)
                tok = TT("vector", st_.t[:], st_.t[:], ft.t[:], ALU.add,
                         waits=st_.slot[0].r() + ft.slot[0].r())
                st_.slot[0].set_w(tok)
                ft.slot[0].add_r(tok)
                xt = x_t[ot % 2]
                tok = TT("gpsimd", st_.t[:], st_.t[:], xt.t[:], ALU.add,
                         waits=st_.slot[0].r() + xt.slot[0].r())
                st_.slot[0].set_w(tok)
                xt.slot[0].add_r(tok)
                tok = b.dma(("sync", "gpsimd")[ot % 2],
                            out_fm[ot * P:(ot + 1) * P, :], st_.t[:],
                            waits=st_.slot[0].r())
                st_.slot[0].add_r(tok)
                out_toks.append(tok)

            b.op("sync", lambda E: E.nop(), waits=out_toks)

        with nc.Block() as blk:
            b.emit(blk)

        counts = {e: b.ctr[e].n for e in ENGS}
        counts.update({f"dma_{e}": c.n // 16 for e, c in b.dctr.items()})
        print("kernel instruction counts:", counts)

    return nc


_NC_CACHE = {}


def get_nc():
    key = DEBUG
    if key not in _NC_CACHE:
        _NC_CACHE[key] = build_nc()
    return _NC_CACHE[key]


def retile_w(w):
    wt = np.asarray(w, np.float32).T
    t = wt.reshape(8, 4, 128, 32, 128).transpose(3, 0, 2, 1, 4)
    return np.ascontiguousarray(t.reshape(32 * 8 * 128, 512))


def make_in_maps(inputs):
    x = np.ascontiguousarray(np.asarray(inputs["L0_attn_hidden_states"],
                                        np.float32))
    wq = retile_w(inputs["L0_attn_wq"])
    wk = retile_w(inputs["L0_attn_wk"])
    wv = retile_w(inputs["L0_attn_wv"])
    wo = retile_w(inputs["L0_attn_wo"])
    shared = {
        "wqT": wq, "wkT": wk, "wvT": wv, "woT": wo,
        "bq": np.asarray(inputs["L0_attn_bq"], np.float32),
        "bk": np.asarray(inputs["L0_attn_bk"], np.float32),
        "bv": np.asarray(inputs["L0_attn_bv"], np.float32),
        "bo": np.asarray(inputs["L0_attn_bo"], np.float32),
        "qw": np.asarray(inputs["L0_ffn_qweight"], np.int32),
        "qz": np.asarray(inputs["L0_ffn_qzeros"], np.int32),
        "sc": np.asarray(inputs["L0_ffn_scales"], np.float32),
        "fb": np.asarray(inputs["L0_ffn_bias"], np.float32),
    }
    maps = []
    for bidx in range(x.shape[0]):
        m = dict(shared)
        m["x_fm"] = np.ascontiguousarray(x[bidx].T)
        maps.append(m)
    return maps


def kernel(**inputs):
    nc = get_nc()
    in_maps = make_in_maps(inputs)
    res = run_bass_kernel_spmd(nc, in_maps, core_ids=list(range(len(in_maps))))
    out = np.stack([np.asarray(r["out_fm"]).T for r in res.results])
    return np.ascontiguousarray(out).astype(np.float32)


# version 08
# speedup vs baseline: 1.4612x; 1.4612x over previous
import numpy as np
from contextlib import ExitStack

import concourse.bass as bass
import concourse.mybir as mybir
from concourse.bass_utils import run_bass_kernel_spmd

F32 = mybir.dt.float32
F16 = mybir.dt.float16
BF16 = mybir.dt.bfloat16
I32 = mybir.dt.int32
AF = mybir.ActivationFunctionType
ALU = mybir.AluOpType
AX = mybir.AxisListType

P = 128
T = 2048
D = 4096
NKC = D // P
H = 32
DH = 128
NIT = T // P
TOPK = 409
C_BIAS = 40.0
INV_SQRT_DH = float(1.0 / np.sqrt(DH))
ALPHA_MU = float(1.0 / (T * np.sqrt(DH)))
CM = float((0.8427 ** 2) / (DH * DH))
W_NEWTON = float(0.001747 / 0.8427)
EPS = 1e-5
N_NEWTON = 1

DEBUG = False

ENGS = ("tensor", "vector", "scalar", "gpsimd", "sync")


class Ctr:
    __slots__ = ("h", "n")

    def __init__(self, h):
        self.h = h
        self.n = 0


class Slot:
    __slots__ = ("ready", "users")

    def __init__(self, seed=None):
        self.ready = None
        self.users = list(seed) if seed else []

    def r(self):
        return [self.ready] if self.ready else []

    def w(self):
        ws = list(self.users)
        if self.ready:
            ws.append(self.ready)
        return ws

    def set_w(self, tok):
        self.ready = tok
        self.users = []

    def add_r(self, tok):
        self.users.append(tok)


class Buf:
    def __init__(self, t, slots=1, seed=None):
        self.t = t
        self.slot = [Slot(seed) for _ in range(slots)]


class KB:
    def __init__(self, nc, stack):
        self.nc = nc
        self.ops = {e: [] for e in ENGS}
        self.ctr = {e: Ctr(stack.enter_context(nc.semaphore(f"c_{e}"))) for e in ENGS}
        self.dctr = {e: Ctr(stack.enter_context(nc.semaphore(f"d_{e}")))
                     for e in ("sync", "scalar", "gpsimd")}
        self.dma_ctrs = set(self.dctr.values())
        self.gate = {}
        self.lastw = {e: {} for e in ENGS}

    def barrier_toks(self):
        out = [(c, c.n) for c in self.ctr.values() if c.n > 0]
        out += [(c, c.n) for c in self.dctr.values() if c.n > 0]
        return out

    def op(self, eng, build, waits=(), ctr=None, amt=1):
        ctr = ctr or self.ctr[eng]
        is_dma = ctr in self.dma_ctrs
        ws_in = list(waits)
        if is_dma and ctr in self.gate:
            ws_in.append(self.gate[ctr])
        ws = []
        seen = self.lastw[eng]
        best = {}
        waited_dma = []
        for (c, v) in ws_in:
            if v is None:
                continue
            if c in self.dma_ctrs:
                if is_dma and c is ctr:
                    continue
                v = c.n
                waited_dma.append(c)
            if v > best.get(c, 0):
                best[c] = v
        for c, v in best.items():
            if v > seen.get(c, 0):
                seen[c] = v
                ws.append((c, v))
        ctr.n += amt
        tok = (ctr, ctr.n)
        for c in waited_dma:
            self.gate[c] = tok
        self.ops[eng].append((ws, build, ctr.h, amt))
        return tok

    def dma(self, eng, out_ap, in_ap, waits=(), sctr=None, **kw):
        return self.op(eng, lambda E: E.dma_start(out=out_ap, in_=in_ap, **kw),
                       waits, ctr=(sctr or self.dctr[eng]), amt=16)

    def emit(self, blk):
        for e in ENGS:
            ops = self.ops[e]
            if not ops:
                continue

            def body(eng, ops=ops):
                for ws, build, sem, amt in ops:
                    for (c, v) in ws:
                        eng.wait_ge(c.h, v)
                    inst = build(eng)
                    if inst is not None:
                        inst.then_inc(sem, amt)

            getattr(blk, e)(body)


def build_nc():
    nc = bass.Bass("TRN2", target_bir_lowering=False)

    x_fm = nc.dram_tensor("x_fm", [D, T], F32, kind="ExternalInput").ap()
    wT = {k: nc.dram_tensor(f"w{k}T", [H * 8 * P, 4 * P], F32,
                            kind="ExternalInput").ap()
          for k in ("q", "k", "v", "o")}
    bias_d = {k: nc.dram_tensor(f"b{k}", [D], F32, kind="ExternalInput").ap()
              for k in ("q", "k", "v", "o")}
    qw_d = nc.dram_tensor("qw", [D // 8, D], I32, kind="ExternalInput").ap()
    qz_d = nc.dram_tensor("qz", [32, 512], I32, kind="ExternalInput").ap()
    sc_d = nc.dram_tensor("sc", [32, D], F32, kind="ExternalInput").ap()
    fb_d = nc.dram_tensor("fb", [D], F32, kind="ExternalInput").ap()

    out_fm = nc.dram_tensor("out_fm", [D, T], F32, kind="ExternalOutput").ap()

    ffn_scr = nc.dram_tensor("ffn_scr", [D, T], F32, kind="Internal").ap()
    ctx_scr = nc.dram_tensor("ctx_scr", [D, T], BF16, kind="Internal").ap()
    zs_dram = nc.dram_tensor("zs_dram", [32, D], F32, kind="Internal").ap()

    dbg = {}
    if DEBUG:
        for name, shape in [("dbg_xn", [D, T]), ("dbg_qT", [DH, T]),
                            ("dbg_kT", [DH, T]), ("dbg_v", [T, DH]),
                            ("dbg_s", [P, T]), ("dbg_t", [P, 16]),
                            ("dbg_p", [P, T]), ("dbg_ctx", [DH, T]),
                            ("dbg_ffn", [P, T]), ("dbg_zs", [32, D])]:
            dbg[name] = nc.dram_tensor(name, shape, F32, kind="ExternalOutput").ap()

    with ExitStack() as stack:
        b = KB(nc, stack)

        def sb(name, shape, dt, slots=1, st=None):
            t = (st or stack).enter_context(nc.sbuf_tensor(name, shape, dt))
            return Buf(t, slots, seed=b.barrier_toks())

        def dsem(name):
            return Ctr(stack.enter_context(nc.semaphore(name)))

        def ps(name, shape, dt, slots=1, st=None):
            t = (st or stack).enter_context(nc.psum_tensor(name, shape, dt))
            return Buf(t, slots, seed=b.barrier_toks())

        xn = sb("xn", [P, NKC, T], BF16, slots=NKC)
        ident = sb("ident", [P, P], BF16)
        iota_s = sb("iota_s", [P, P], I32)
        ones_col = sb("ones_col", [P, 1], BF16)
        ones_row = sb("ones_row", [1, P], BF16)
        cneg = sb("cneg", [P, 1], F32)
        eps_t = sb("eps_t", [1, 1], F32)
        bqh = sb("bqh", [P, H], F32)
        bkh = sb("bkh", [P, H], F32)
        bvh = sb("bvh", [P, H], F32)
        boh = sb("boh", [P, H], F32)
        fbh = sb("fbh", [P, H], F32)
        WRING = 8
        wring = {pr: [sb(f"wr_{pr}_{i}", [P, 4, P], BF16) for i in range(WRING)]
                 for pr in ("q", "k", "v")}
        wsem = {pr: [dsem(f"ws_{pr}_{i}") for i in range(WRING)]
                for pr in ("q", "k", "v")}

        ksum = sb("ksum", [P, 1], F32)
        ksum_bf = sb("ksum_bf", [P, 1], BF16)
        k2sum = sb("k2sum", [P, 1], F32)
        el_a = sb("el_a", [P, 1], F32)
        el_b = sb("el_b", [P, 1], F32)
        el_bf = sb("el_bf", [P, 1], BF16)
        kap_sb = sb("kap_sb", [1, 1], BF16)
        kapbc = sb("kapbc", [P, 1], F32)
        sgs = sb("sgs", [P, 16], F32, slots=16)
        term = sb("term", [P, 16], F32, slots=16)
        tcol = sb("tcol", [P, 16], F32, slots=16)
        wcol = sb("wcol", [P, 16], F32, slots=16)
        cnt = sb("cnt", [P, 16], F32, slots=16)
        tau = sb("tau", [P, 16], F32)
        tau2 = sb("tau2", [P, 16], F32, slots=16)
        biasall = sb("biasall", [P, 16], F32)
        zacc = sb("zacc", [P, 16], F32, slots=16)
        zinv = sb("zinv", [P, 16], F32, slots=16)
        diag = [sb(f"diag{i}", [P, P], BF16) for i in range(2)]

        def TS(eng, out, in0, s1, s2=None, op0=ALU.bypass, op1=None,
               accum=None, waits=()):
            def f(E, out=out, in0=in0, s1=s1, s2=s2, op0=op0, op1=op1, accum=accum):
                kw = dict(out=out, in0=in0, scalar1=s1, scalar2=s2, op0=op0)
                if op1 is not None:
                    kw["op1"] = op1
                if accum is not None:
                    kw["accum_out"] = accum
                return E.tensor_scalar(**kw)
            return b.op(eng, f, waits)

        def TT(eng, out, in0, in1, alu, waits=()):
            return b.op(eng, lambda E, out=out, in0=in0, in1=in1, alu=alu:
                        E.tensor_tensor(out=out, in0=in0, in1=in1, op=alu), waits)

        def STT(eng, out, in0, s, in1, op0, op1, accum=None, waits=()):
            def f(E, out=out, in0=in0, s=s, in1=in1, op0=op0, op1=op1, accum=accum):
                kw = dict(out=out, in0=in0, scalar=s, in1=in1, op0=op0, op1=op1)
                if accum is not None:
                    kw["accum_out"] = accum
                return E.scalar_tensor_tensor(**kw)
            return b.op(eng, f, waits)

        def ACT(out, in_, func, bias=None, scale=1.0, accum=None, waits=(), eng="scalar"):
            def f(E, out=out, in_=in_, func=func, bias=bias, scale=scale, accum=accum):
                kw = dict(out=out, in_=in_, func=func, scale=scale)
                if bias is not None:
                    kw["bias"] = bias
                if accum is not None:
                    kw["accum_out"] = accum
                return E.activation(**kw)
            return b.op(eng, f, waits)

        def MM(out, lhsT, rhs, start, stop, waits=()):
            return b.op("tensor",
                        lambda E, out=out, l=lhsT, r=rhs, st=start, sp=stop:
                        E.matmul(out, l, r, start=st, stop=sp), waits)

        def TRANS(out, in_, waits=()):
            return b.op("tensor",
                        lambda E, out=out, i=in_: E.transpose(out, i, ident.t[:]),
                        waits)

        def RED(eng, out, in_, alu, waits=()):
            return b.op(eng, lambda E, out=out, i=in_, a=alu:
                        E.tensor_reduce(out, i, axis=AX.X, op=a), waits)

        def RECIP(out, in_, waits=()):
            return b.op("vector", lambda E, o=out, i=in_: E.reciprocal(o, i), waits)

        tok = b.op("gpsimd", lambda E: E.iota(iota_s.t[:], pattern=[[-1, P]],
                                              base=0, channel_multiplier=1))
        iota_s.slot[0].set_w(tok)
        for buf, val in ((ones_col, 1.0), (ones_row, 1.0), (cneg, -C_BIAS),
                         (eps_t, EPS)):
            tok = b.op("gpsimd", lambda E, bu=buf, v=val: E.memset(bu.t[:], v),
                       waits=buf.slot[0].w())
            buf.slot[0].set_w(tok)
        tok = TS("vector", ident.t[:], iota_s.t[:], 0, op0=ALU.is_equal,
                 waits=iota_s.slot[0].r() + ident.slot[0].w())
        ident.slot[0].set_w(tok)
        iota_s.slot[0].add_r(tok)
        pbias = ExitStack()
        ident_f = sb("ident_f", [P, P], F32, st=pbias)
        tok = TS("vector", ident_f.t[:], iota_s.t[:], 0, op0=ALU.is_equal,
                 waits=iota_s.slot[0].r() + ident_f.slot[0].w())
        ident_f.slot[0].set_w(tok)
        iota_s.slot[0].add_r(tok)

        bstage = sb("bstage", [32, 5, P], F32, st=pbias)
        bias_ps = ps("bias_ps", [P, 5 * H], F32, st=pbias)
        bias_list = ((bias_d["q"], bqh), (bias_d["k"], bkh), (bias_d["v"], bvh),
                     (bias_d["o"], boh), (fb_d, fbh))
        for i, (dram, buf) in enumerate(bias_list):
            view = dram.rearrange("(h p) -> h p", p=P)
            tok = b.dma("sync", bstage.t[:, i, :], view, waits=bstage.slot[0].w())
            bstage.slot[0].set_w(tok)
        for i, (dram, buf) in enumerate(bias_list):
            w1 = bstage.slot[0].r() + ident_f.slot[0].r()
            if i == 0:
                w1 = w1 + bias_ps.slot[0].w()
            tok = b.op("tensor",
                       lambda E, i=i: E.transpose(bias_ps.t[:, i * H:(i + 1) * H],
                                                  bstage.t[:, i, :],
                                                  ident_f.t[0:32, 0:32]),
                       waits=w1)
            bstage.slot[0].add_r(tok)
            ident_f.slot[0].add_r(tok)
            if i == 4:
                bias_ps.slot[0].set_w(tok)
        for i, (dram, buf) in enumerate(bias_list):
            tok = TS("vector", buf.t[:], bias_ps.t[:, i * H:(i + 1) * H], 0.0,
                     op0=ALU.add, waits=bias_ps.slot[0].r() + buf.slot[0].w())
            buf.slot[0].set_w(tok)
            bias_ps.slot[0].add_r(tok)
        pbias.close()

        p0 = ExitStack()
        qz_sb = sb("qz_sb", [32, 512], I32, st=p0)
        sc_sb = sb("sc_sb", [32, D], F32, st=p0)
        zi32 = sb("zi32", [32, D], I32, st=p0)
        zs_sb = sb("zs_sb", [32, D], F32, st=p0)
        tok = b.dma("sync", qz_sb.t[:], qz_d, waits=qz_sb.slot[0].w())
        qz_sb.slot[0].set_w(tok)
        tok = b.dma("sync", sc_sb.t[:], sc_d, waits=sc_sb.slot[0].w())
        sc_sb.slot[0].set_w(tok)
        zview = zi32.t[:].rearrange("p (c e) -> p c e", e=8)
        for j in range(8):
            tok = TS("vector", zview[:, :, j], qz_sb.t[:], 4 * j, 15,
                     op0=ALU.logical_shift_right, op1=ALU.bitwise_and,
                     waits=qz_sb.slot[0].r() + zi32.slot[0].w())
            zi32.slot[0].set_w(tok)
            qz_sb.slot[0].add_r(tok)
        tok = TT("vector", zs_sb.t[:], zi32.t[:], sc_sb.t[:], ALU.mult,
                 waits=zi32.slot[0].r() + sc_sb.slot[0].r() + zs_sb.slot[0].w())
        zs_sb.slot[0].set_w(tok)
        zi32.slot[0].add_r(tok)
        sc_sb.slot[0].add_r(tok)
        zs_tok = b.dma("sync", zs_dram, zs_sb.t[:], waits=zs_sb.slot[0].r())
        zs_sb.slot[0].add_r(zs_tok)
        if DEBUG:
            tk = b.dma("gpsimd", dbg["dbg_zs"], zs_sb.t[:], waits=zs_sb.slot[0].r())
            zs_sb.slot[0].add_r(tk)
        p0.close()

        with ExitStack() as pa:
            xf32 = [sb(f"xf32_{i}", [P, T], F32, st=pa) for i in range(3)]
            xfsem = [dsem(f"xfs{i}") for i in range(3)]
            xfq = ("sync", "scalar", "gpsimd")
            x2b = [sb(f"x2b_{i}", [P, T], BF16, st=pa) for i in range(2)]
            mu_row = sb("mu_row", [1, 512], F32, st=pa)
            var_row = sb("var_row", [1, 512], F32, st=pa)
            std_row = sb("std_row", [1, 512], F32, st=pa)
            musq = var_row
            r_row = std_row
            mu_bfr = sb("mu_bfr", [1, T], BF16, st=pa)
            r_bfr = sb("r_bfr", [1, T], BF16, st=pa)

            with ExitStack() as pstats:
                sx_ps = [ps(f"sx_ps{i}", [1, 512], F32, st=pstats) for i in range(4)]
                sx2_ps = [ps(f"sx2_ps{i}", [1, 512], F32, st=pstats) for i in range(4)]

                for kc in range(NKC):
                    bf_ = kc % 3
                    bf2 = kc % 2
                    tok = b.dma(xfq[bf_], xf32[bf_].t[:], x_fm[kc * P:(kc + 1) * P, :],
                                waits=xf32[bf_].slot[0].w(), sctr=xfsem[bf_])
                    xf32[bf_].slot[0].set_w(tok)
                    tok = ACT(xn.t[:, kc, :], xf32[bf_].t[:], AF.Copy, bias=0.0,
                              waits=xf32[bf_].slot[0].r() + xn.slot[kc].w())
                    xn.slot[kc].set_w(tok)
                    xf32[bf_].slot[0].add_r(tok)
                    tok = STT("vector", x2b[bf2].t[:], xn.t[:, kc, :], 0.0,
                              xn.t[:, kc, :], ALU.bypass, ALU.mult,
                              waits=xn.slot[kc].r() + x2b[bf2].slot[0].w())
                    x2b[bf2].slot[0].set_w(tok)
                    xn.slot[kc].add_r(tok)
                    for n in range(4):
                        w1 = ones_col.slot[0].r() + xn.slot[kc].r()
                        if kc == 0:
                            w1 = w1 + sx_ps[n].slot[0].w()
                        tok = MM(sx_ps[n].t[:], ones_col.t[:],
                                 xn.t[:, kc, n * 512:(n + 1) * 512],
                                 start=(kc == 0), stop=(kc == NKC - 1), waits=w1)
                        xn.slot[kc].add_r(tok)
                        ones_col.slot[0].add_r(tok)
                        if kc == NKC - 1:
                            sx_ps[n].slot[0].set_w(tok)
                    for n in range(4):
                        w1 = ones_col.slot[0].r() + x2b[bf2].slot[0].r()
                        if kc == 0:
                            w1 = w1 + sx2_ps[n].slot[0].w()
                        tok = MM(sx2_ps[n].t[:], ones_col.t[:],
                                 x2b[bf2].t[:, n * 512:(n + 1) * 512],
                                 start=(kc == 0), stop=(kc == NKC - 1), waits=w1)
                        x2b[bf2].slot[0].add_r(tok)
                        ones_col.slot[0].add_r(tok)
                        if kc == NKC - 1:
                            sx2_ps[n].slot[0].set_w(tok)

                for n in range(4):
                    c0, c1 = n * 512, (n + 1) * 512
                    tok = TS("vector", mu_row.t[:], sx_ps[n].t[:], 1.0 / D,
                             op0=ALU.mult,
                             waits=sx_ps[n].slot[0].r() + mu_row.slot[0].w())
                    mu_row.slot[0].set_w(tok)
                    sx_ps[n].slot[0].add_r(tok)
                    tok = STT("vector", musq.t[:], mu_row.t[:], 0.0,
                              mu_row.t[:], ALU.bypass, ALU.mult,
                              waits=mu_row.slot[0].r() + musq.slot[0].w())
                    musq.slot[0].set_w(tok)
                    mu_row.slot[0].add_r(tok)
                    tok = TS("vector", mu_bfr.t[:, c0:c1], mu_row.t[:], 0.0,
                             op0=ALU.add,
                             waits=mu_row.slot[0].r() + mu_bfr.slot[0].w())
                    mu_bfr.slot[0].set_w(tok)
                    mu_row.slot[0].add_r(tok)
                    tok = STT("vector", var_row.t[:], sx2_ps[n].t[:], 1.0 / D,
                              musq.t[:], ALU.mult, ALU.subtract,
                              waits=sx2_ps[n].slot[0].r() + musq.slot[0].r()
                              + var_row.slot[0].w())
                    var_row.slot[0].set_w(tok)
                    sx2_ps[n].slot[0].add_r(tok)
                    musq.slot[0].add_r(tok)
                    tok = ACT(std_row.t[:], var_row.t[:], AF.Sqrt,
                              bias=eps_t.t[:],
                              waits=var_row.slot[0].r() + eps_t.slot[0].r()
                              + std_row.slot[0].w())
                    std_row.slot[0].set_w(tok)
                    var_row.slot[0].add_r(tok)
                    eps_t.slot[0].add_r(tok)
                    tok = RECIP(r_row.t[:], std_row.t[:],
                                waits=std_row.slot[0].r() + r_row.slot[0].w())
                    r_row.slot[0].set_w(tok)
                    std_row.slot[0].add_r(tok)
                    tok = TS("vector", r_bfr.t[:, c0:c1], r_row.t[:], 0.0,
                             op0=ALU.add,
                             waits=r_row.slot[0].r() + r_bfr.slot[0].w())
                    r_bfr.slot[0].set_w(tok)
                    r_row.slot[0].add_r(tok)

            with ExitStack() as pbc:
                mu_bc = ps("mu_bc", [P, T], F32, st=pbc)
                r_bc = ps("r_bc", [P, T], F32, st=pbc)
                for n in range(4):
                    c0, c1 = n * 512, (n + 1) * 512
                    tok = MM(mu_bc.t[:, c0:c1], ones_row.t[:], mu_bfr.t[:, c0:c1],
                             start=True, stop=True,
                             waits=ones_row.slot[0].r() + mu_bfr.slot[0].r()
                             + mu_bc.slot[0].w())
                    mu_bc.slot[0].set_w(tok)
                    mu_bfr.slot[0].add_r(tok)
                    ones_row.slot[0].add_r(tok)
                    tok = MM(r_bc.t[:, c0:c1], ones_row.t[:], r_bfr.t[:, c0:c1],
                             start=True, stop=True,
                             waits=ones_row.slot[0].r() + r_bfr.slot[0].r()
                             + r_bc.slot[0].w())
                    r_bc.slot[0].set_w(tok)
                    r_bfr.slot[0].add_r(tok)
                    ones_row.slot[0].add_r(tok)
                for kc in range(NKC):
                    bf_ = kc % 3
                    scr = xf32[bf_]
                    tok = TT("vector", scr.t[:], xn.t[:, kc, :], mu_bc.t[:],
                             ALU.subtract,
                             waits=xn.slot[kc].r() + mu_bc.slot[0].r()
                             + scr.slot[0].w())
                    scr.slot[0].set_w(tok)
                    xn.slot[kc].add_r(tok)
                    mu_bc.slot[0].add_r(tok)
                    tok = TT("vector", xn.t[:, kc, :], scr.t[:], r_bc.t[:], ALU.mult,
                             waits=scr.slot[0].r() + r_bc.slot[0].r()
                             + xn.slot[kc].w())
                    xn.slot[kc].set_w(tok)
                    scr.slot[0].add_r(tok)
                    r_bc.slot[0].add_r(tok)
                    if DEBUG:
                        tk = b.dma("gpsimd", dbg["dbg_xn"][kc * P:(kc + 1) * P, :],
                                   xn.t[:, kc, :], waits=xn.slot[kc].r())
                        xn.slot[kc].add_r(tk)

        WQUEUE = {"q": "gpsimd", "k": "gpsimd", "v": "gpsimd"}

        def prefetch_head(h):
            toks = {}
            for pr in ("q", "k", "v"):
                lst = []
                for g in range(8):
                    sl = wring[pr][g]
                    base = (h * 8 + g) * P
                    view = wT[pr][base:base + P, :].rearrange(
                        "p (j c) -> p j c", c=P)
                    tok = b.dma(WQUEUE[pr], sl.t[:], view,
                                waits=sl.slot[0].w(), sctr=wsem[pr][g])
                    sl.slot[0].set_w(tok)
                    lst.append(tok)
                toks[pr] = lst
            return toks

        wtoks = prefetch_head(0)

        ffn_out_toks = [[] for _ in range(NKC)]
        with ExitStack() as pb:
            ffn_ps = [ps(f"ffn_ps{i}", [P, T], F32, st=pb) for i in range(2)]
            W_sb = [sb(f"W_sb{i}", [P, NKC, 256], BF16, slots=NKC, st=pb)
                    for i in range(2)]
            qwr = [sb(f"qwr{i}", [P, 2, 256], I32, st=pb) for i in range(2)]
            zbc = [sb(f"zbc{i}", [P, 2, 256], F32, st=pb) for i in range(2)]
            sbc = [sb(f"sbc{i}", [P, 2, 256], F32, st=pb) for i in range(2)]
            qwsem = [dsem(f"qws{i}") for i in range(2)]
            zbsem = [dsem(f"zbs{i}") for i in range(2)]
            sbsem = [dsem(f"sbs{i}") for i in range(2)]
            wi32 = sb("wi32", [P, 2, 256], I32, st=pb)
            shiftc = sb("shiftc", [P, 1], I32, st=pb)
            tok = b.op("gpsimd", lambda E: E.iota(shiftc.t[:], pattern=[[0, 1]],
                                                  base=0, channel_multiplier=4),
                       waits=shiftc.slot[0].w())
            shiftc.slot[0].set_w(tok)
            tok = TS("vector", shiftc.t[:], shiftc.t[:], 31, op0=ALU.bitwise_and,
                     waits=shiftc.slot[0].r())
            shiftc.slot[0].set_w(tok)
            wtmp = sb("wtmp", [P, 2, 256], F32, st=pb)
            fstage = [sb(f"fstage{i}", [P, 512], F32, st=pb) for i in range(2)]
            tok = TT("vector", fbh.t[:], fbh.t[:], boh.t[:], ALU.add,
                     waits=fbh.slot[0].r() + boh.slot[0].r())
            fbh.slot[0].set_w(tok)
            boh.slot[0].add_r(tok)

            def dequant_half(b2, dst, kps):
                co = b2 * 256
                for kp in kps:
                    sl = (kp // 2) % 2
                    sq = (kp // 2) % 2
                    for k in range(2):
                        kc = kp + k
                        src = qw_d[kc * 16:(kc + 1) * 16, co:co + 256]
                        src = src.rearrange("r (u c) -> r u c",
                                            u=1).broadcast_to([16, 8, 256])
                        tok = b.dma("sync", qwr[sq].t[:, k, :], src,
                                    waits=qwr[sq].slot[0].w(), sctr=qwsem[sq])
                        qwr[sq].slot[0].set_w(tok)
                        zsrc = zs_dram[kc:kc + 1, co:co + 256]
                        zsrc = zsrc.rearrange("r (u c) -> r u c",
                                              u=1).broadcast_to([1, P, 256])
                        tok = b.dma("scalar", zbc[sl].t[:, k, :], zsrc,
                                    waits=zbc[sl].slot[0].w() + [zs_tok],
                                    sctr=zbsem[sl])
                        zbc[sl].slot[0].set_w(tok)
                        ssrc = sc_d[kc:kc + 1, co:co + 256]
                        ssrc = ssrc.rearrange("r (u c) -> r u c",
                                              u=1).broadcast_to([1, P, 256])
                        tok = b.dma("scalar", sbc[sl].t[:, k, :], ssrc,
                                    waits=sbc[sl].slot[0].w(), sctr=sbsem[sl])
                        sbc[sl].slot[0].set_w(tok)

                    tok = TS("vector", wi32.t[:], qwr[sq].t[:], shiftc.t[:], 15,
                             op0=ALU.logical_shift_right, op1=ALU.bitwise_and,
                             waits=qwr[sq].slot[0].r() + shiftc.slot[0].r()
                             + wi32.slot[0].w())
                    wi32.slot[0].set_w(tok)
                    qwr[sq].slot[0].add_r(tok)
                    shiftc.slot[0].add_r(tok)
                    tok = TT("vector", wtmp.t[:], wi32.t[:], sbc[sl].t[:], ALU.mult,
                             waits=wi32.slot[0].r() + sbc[sl].slot[0].r()
                             + wtmp.slot[0].w())
                    wtmp.slot[0].set_w(tok)
                    wi32.slot[0].add_r(tok)
                    sbc[sl].slot[0].add_r(tok)
                    tok = TT("vector", dst.t[:, kp:kp + 2, :], wtmp.t[:],
                             zbc[sl].t[:], ALU.subtract,
                             waits=wtmp.slot[0].r() + zbc[sl].slot[0].r()
                             + dst.slot[kp].w() + dst.slot[kp + 1].w())
                    dst.slot[kp].set_w(tok)
                    dst.slot[kp + 1].set_w(tok)
                    wtmp.slot[0].add_r(tok)
                    zbc[sl].slot[0].add_r(tok)

            dequant_half(0, W_sb[0], range(0, NKC, 2))
            for b2 in range(16):
                cur = W_sb[b2 % 2]
                for ot_l in range(2):
                    ot = 2 * b2 + ot_l
                    fps = ffn_ps[ot % 2]
                    for kc in range(NKC):
                        lhs = cur.t[:, kc, ot_l * P:(ot_l + 1) * P]
                        for n in range(4):
                            w1 = cur.slot[kc].r() + xn.slot[kc].r()
                            if kc == 0 and n == 0:
                                w1 = w1 + fps.slot[0].w()
                            tok = MM(fps.t[:, n * 512:(n + 1) * 512], lhs,
                                     xn.t[:, kc, n * 512:(n + 1) * 512],
                                     start=(kc == 0), stop=(kc == NKC - 1),
                                     waits=w1)
                            cur.slot[kc].add_r(tok)
                            xn.slot[kc].add_r(tok)
                            if kc == NKC - 1 and n == 3:
                                fps.slot[0].set_w(tok)
                    if b2 + 1 < 16:
                        dequant_half(b2 + 1, W_sb[(b2 + 1) % 2],
                                     range(ot_l * 16, (ot_l + 1) * 16, 2))
                    for hf in range(4):
                        st_ = fstage[hf % 2]
                        c0, c1 = hf * 512, (hf + 1) * 512
                        tok = ACT(st_.t[:], fps.t[:, c0:c1], AF.Copy, bias=0.0,
                                  waits=fps.slot[0].r() + st_.slot[0].w())
                        st_.slot[0].set_w(tok)
                        fps.slot[0].add_r(tok)
                        tok = b.dma("sync", ffn_scr[ot * P:(ot + 1) * P, c0:c1],
                                    st_.t[:], waits=st_.slot[0].r())
                        st_.slot[0].add_r(tok)
                        ffn_out_toks[ot].append(tok)
                        if DEBUG and ot == 0:
                            tk = b.dma("gpsimd", dbg["dbg_ffn"][:, c0:c1], st_.t[:],
                                       waits=st_.slot[0].r())
                            st_.slot[0].add_r(tk)

        ctx_dma_toks = [None] * H
        ctxsem = [dsem(f"ctxs{i}") for i in range(2)]
        with ExitStack() as pc:
            tr_ps = [ps(f"tr_ps{i}", [P, 1024], F32, st=pc) for i in range(2)]
            s_ps = ps("s_ps", [P, 1024], F32, st=pc)
            ctx_ps = ps("ctx_ps", [P, 512], F32, st=pc)
            st_ps = ps("st_ps", [P, 512], F32, st=pc)

            qT = sb("qT", [P, T], BF16, st=pc)
            kT = sb("kT", [P, T], BF16, st=pc)
            vT = sb("vT", [P, T], BF16, st=pc)
            v_sb = sb("v_sb", [P, T], BF16, slots=NIT, st=pc)
            e_sb = [sb(f"e_sb{i}", [P, T], BF16, st=pc) for i in range(2)]
            p_sb = [sb(f"p_sb{i}", [P, T], BF16, st=pc) for i in range(2)]
            pT = sb("pT", [P, 16 * 512], BF16, slots=8, st=pc)
            ctx_sb = sb("ctx_sb", [P, T], BF16, slots=4, st=pc)

            pT_v = pT.t[:].rearrange("p (jc i) -> p jc i", i=512)
            tr_v = [tr_ps[i].t[:].rearrange("p (j c) -> p j c", c=P)
                    for i in range(2)]
            v_sb_v = v_sb.t[:].rearrange("p (jc c) -> p jc c", c=P)
            st_v = st_ps.t[:].rearrange("p (c e) -> p c e", e=2)

            for h in range(H):
                def proj(pr, evac_dst, bias_buf, dst_slot):
                    for kc in range(NKC):
                        sl = wring[pr][kc // 4]
                        lhs = sl.t[:, kc % 4, :]
                        for n in range(4):
                            pst = tr_ps[0] if n < 2 else tr_ps[1]
                            col = (n % 2) * 512
                            w1 = [wtoks[pr][kc // 4]] + xn.slot[kc].r()
                            if kc == 0 and n % 2 == 0:
                                w1 = w1 + pst.slot[0].w()
                            tok = MM(pst.t[:, col:col + 512], lhs,
                                     xn.t[:, kc, n * 512:(n + 1) * 512],
                                     start=(kc == 0), stop=(kc == NKC - 1), waits=w1)
                            sl.slot[0].add_r(tok)
                            xn.slot[kc].add_r(tok)
                            if kc == NKC - 1:
                                pst.slot[0].set_w(tok)
                    for half in range(2):
                        pst = tr_ps[half]
                        tok = TS("vector", evac_dst[:, half * 1024:(half + 1) * 1024],
                                 pst.t[:], bias_buf.t[:, h:h + 1], op0=ALU.add,
                                 waits=pst.slot[0].r() + bias_buf.slot[0].r()
                                 + dst_slot.w())
                        dst_slot.set_w(tok)
                        pst.slot[0].add_r(tok)
                        bias_buf.slot[0].add_r(tok)

                q2 = vT
                proj("q", qT.t[:], bqh, qT.slot[0])
                proj("k", kT.t[:], bkh, kT.slot[0])
                proj("v", vT.t[:], bvh, vT.slot[0])

                if h + 1 < H:
                    wtoks = prefetch_head(h + 1)
                if h == H - 1:
                    for hc in range(NKC - 1):
                        qn = ("scalar", "sync")[hc % 2]
                        tok = b.dma(qn, xn.t[:, hc, :],
                                    ctx_scr[hc * P:(hc + 1) * P, :],
                                    waits=xn.slot[hc].w() + [ctx_dma_toks[hc]],
                                    sctr=ctxsem[hc % 2])
                        xn.slot[hc].set_w(tok)

                for rnd in range(2):
                    tp = tr_ps[rnd]
                    tvv = tr_v[rnd]
                    for j8 in range(8):
                        jc = rnd * 8 + j8
                        w1 = vT.slot[0].r() + ident.slot[0].r()
                        if j8 == 0:
                            w1 = w1 + tp.slot[0].w()
                        tok = MM(tvv[:, j8, :], vT.t[:, jc * P:(jc + 1) * P],
                                 ident.t[:], start=True, stop=True, waits=w1)
                        if j8 == 7:
                            tp.slot[0].set_w(tok)
                        vT.slot[0].add_r(tok)
                        ident.slot[0].add_r(tok)
                    eng = "vector" if rnd == 0 else "scalar"
                    dst = v_sb_v[:, rnd * 8:(rnd + 1) * 8, :]
                    wsl = []
                    for jj in range(8):
                        wsl += v_sb.slot[rnd * 8 + jj].w()
                    if eng == "vector":
                        tok = TS("vector", dst, tvv[:, :, :], 0.0, op0=ALU.add,
                                 waits=tp.slot[0].r() + wsl)
                    else:
                        tok = ACT(dst, tvv[:, :, :], AF.Copy, bias=0.0,
                                  waits=tp.slot[0].r() + wsl)
                    for jj in range(8):
                        v_sb.slot[rnd * 8 + jj].set_w(tok)
                    tp.slot[0].add_r(tok)

                if DEBUG and h == 1:
                    for nm, bb in (("dbg_qT", qT), ("dbg_kT", kT)):
                        tk = b.dma("gpsimd", dbg[nm], bb.t[:], waits=bb.slot[0].r())
                        bb.slot[0].add_r(tk)

                tok = RED("vector", ksum.t[:], kT.t[:], ALU.add,
                          waits=kT.slot[0].r() + ksum.slot[0].w())
                ksum.slot[0].set_w(tok)
                kT.slot[0].add_r(tok)
                tok = TS("vector", ksum_bf.t[:], ksum.t[:], 0.0, op0=ALU.add,
                         waits=ksum.slot[0].r() + ksum_bf.slot[0].w())
                ksum_bf.slot[0].set_w(tok)
                ksum.slot[0].add_r(tok)
                tok = STT("vector", p_sb[0].t[:], kT.t[:], 0.0, kT.t[:],
                          ALU.bypass, ALU.mult, accum=k2sum.t[:],
                          waits=kT.slot[0].r() + p_sb[0].slot[0].w()
                          + k2sum.slot[0].w())
                p_sb[0].slot[0].set_w(tok)
                k2sum.slot[0].set_w(tok)
                kT.slot[0].add_r(tok)
                tok = STT("vector", q2.t[:], qT.t[:], 0.0, qT.t[:],
                          ALU.bypass, ALU.mult,
                          waits=qT.slot[0].r() + q2.slot[0].w())
                q2.slot[0].set_w(tok)
                qT.slot[0].add_r(tok)
                tok = TS("vector", el_a.t[:], k2sum.t[:], CM / T, op0=ALU.mult,
                         waits=k2sum.slot[0].r() + el_a.slot[0].w())
                el_a.slot[0].set_w(tok)
                k2sum.slot[0].add_r(tok)
                tok = STT("vector", el_b.t[:], ksum.t[:], CM / (T * T), ksum.t[:],
                          ALU.mult, ALU.mult,
                          waits=ksum.slot[0].r() + el_b.slot[0].w())
                el_b.slot[0].set_w(tok)
                ksum.slot[0].add_r(tok)
                tok = TT("vector", el_a.t[:], el_a.t[:], el_b.t[:], ALU.subtract,
                         waits=el_a.slot[0].r() + el_b.slot[0].r())
                el_a.slot[0].set_w(tok)
                el_b.slot[0].add_r(tok)
                tok = TS("vector", el_bf.t[:], el_a.t[:], 0.0, op0=ALU.add,
                         waits=el_a.slot[0].r() + el_bf.slot[0].w())
                el_bf.slot[0].set_w(tok)
                el_a.slot[0].add_r(tok)
                tok = MM(st_ps.t[0:1, 200:201], el_bf.t[:], ones_col.t[:],
                         start=True, stop=True,
                         waits=el_bf.slot[0].r() + ones_col.slot[0].r()
                         + st_ps.slot[0].w())
                st_ps.slot[0].set_w(tok)
                el_bf.slot[0].add_r(tok)
                ones_col.slot[0].add_r(tok)
                tok = TS("vector", kap_sb.t[:], st_ps.t[0:1, 200:201], 0.0,
                         op0=ALU.add,
                         waits=st_ps.slot[0].r() + kap_sb.slot[0].w())
                kap_sb.slot[0].set_w(tok)
                st_ps.slot[0].add_r(tok)
                tok = MM(st_ps.t[:, 210:211], ones_row.t[:], kap_sb.t[:],
                         start=True, stop=True,
                         waits=ones_row.slot[0].r() + kap_sb.slot[0].r()
                         + st_ps.slot[0].w())
                st_ps.slot[0].set_w(tok)
                kap_sb.slot[0].add_r(tok)
                ones_row.slot[0].add_r(tok)
                tok = TS("vector", kapbc.t[:], st_ps.t[:, 210:211], 0.0, op0=ALU.add,
                         waits=st_ps.slot[0].r() + kapbc.slot[0].w())
                kapbc.slot[0].set_w(tok)
                st_ps.slot[0].add_r(tok)

                for it in range(NIT):
                    tok = MM(st_ps.t[:, 2 * it:2 * it + 1],
                             qT.t[:, it * P:(it + 1) * P], ksum_bf.t[:],
                             start=True, stop=True,
                             waits=qT.slot[0].r() + ksum_bf.slot[0].r()
                             + st_ps.slot[0].w())
                    st_ps.slot[0].set_w(tok)
                    qT.slot[0].add_r(tok)
                    ksum_bf.slot[0].add_r(tok)
                    tok = MM(st_ps.t[:, 2 * it + 1:2 * it + 2],
                             q2.t[:, it * P:(it + 1) * P], ones_col.t[:],
                             start=True, stop=True,
                             waits=q2.slot[0].r() + ones_col.slot[0].r()
                             + st_ps.slot[0].w())
                    st_ps.slot[0].set_w(tok)
                    q2.slot[0].add_r(tok)
                    ones_col.slot[0].add_r(tok)

                tok = TS("vector", sgs.t[:], st_v[:, 0:16, 1], kapbc.t[:],
                         op0=ALU.mult,
                         waits=st_ps.slot[0].r() + kapbc.slot[0].r()
                         + sgs.slot[0].w())
                for ii in range(16):
                    sgs.slot[ii].set_w(tok)
                st_ps.slot[0].add_r(tok)
                kapbc.slot[0].add_r(tok)
                tok = ACT(term.t[:], sgs.t[:], AF.Sqrt, bias=0.0,
                          waits=sgs.slot[0].r() + term.slot[0].w())
                for ii in range(16):
                    term.slot[ii].set_w(tok)
                sgs.slot[0].add_r(tok)
                tok = STT("vector", tcol.t[:], st_v[:, 0:16, 0], ALPHA_MU,
                          term.t[:], ALU.mult, ALU.add,
                          waits=st_ps.slot[0].r() + term.slot[0].r()
                          + sum((tcol.slot[ii].w() for ii in range(16)), []))
                for ii in range(16):
                    tcol.slot[ii].set_w(tok)
                st_ps.slot[0].add_r(tok)
                term.slot[0].add_r(tok)
                tok = TS("vector", tcol.t[:], tcol.t[:], -C_BIAS, op0=ALU.add,
                         waits=tcol.slot[0].r())
                for ii in range(16):
                    tcol.slot[ii].set_w(tok)
                tok = TS("vector", wcol.t[:], term.t[:], -W_NEWTON / 2.0,
                         op0=ALU.mult,
                         waits=term.slot[0].r()
                         + sum((wcol.slot[ii].w() for ii in range(16)), []))
                for ii in range(16):
                    wcol.slot[ii].set_w(tok)
                term.slot[0].add_r(tok)
                tok = STT("vector", biasall.t[:], wcol.t[:], float(2 * TOPK - T),
                          tcol.t[:], ALU.mult, ALU.add,
                          waits=wcol.slot[0].r() + tcol.slot[0].r()
                          + biasall.slot[0].w())
                biasall.slot[0].set_w(tok)
                wcol.slot[0].add_r(tok)
                tcol.slot[0].add_r(tok)
                tok = ACT(tau.t[:], tcol.t[:], AF.Exp, bias=0.0,
                          waits=tcol.slot[0].r() + tau.slot[0].w())
                tau.slot[0].set_w(tok)
                for ii in range(16):
                    tcol.slot[ii].add_r(tok)

                def sc_half(it, half):
                    for c in range(2):
                        j0 = half * 1024 + c * 512
                        w1 = qT.slot[0].r() + kT.slot[0].r()
                        if c == 0:
                            w1 = w1 + s_ps.slot[0].w()
                        tok = MM(s_ps.t[:, c * 512:(c + 1) * 512],
                                 qT.t[:, it * P:(it + 1) * P],
                                 kT.t[:, j0:j0 + 512],
                                 start=True, stop=True, waits=w1)
                        qT.slot[0].add_r(tok)
                        kT.slot[0].add_r(tok)
                        if c == 1:
                            s_ps.slot[0].set_w(tok)

                def exp_half(it, half):
                    eb_ = e_sb[it % 2]
                    tok = ACT(eb_.t[:, half * 1024:(half + 1) * 1024],
                              s_ps.t[:], AF.Exp, bias=cneg.t[:],
                              scale=INV_SQRT_DH,
                              waits=s_ps.slot[0].r() + cneg.slot[0].r()
                              + eb_.slot[0].w())
                    eb_.slot[0].set_w(tok)
                    s_ps.slot[0].add_r(tok)
                    cneg.slot[0].add_r(tok)

                def rest2(it):
                    pb_ = p_sb[it % 2]
                    eb_ = e_sb[it % 2]
                    ic = it
                    tok = ACT(q2.t[:], eb_.t[:], AF.Sign, bias=tau.t[:, ic:ic + 1],
                              scale=-1.0, accum=cnt.t[:, ic:ic + 1],
                              waits=eb_.slot[0].r() + tau.slot[0].r()
                              + q2.slot[0].w() + cnt.slot[ic].w())
                    q2.slot[0].set_w(tok)
                    cnt.slot[ic].set_w(tok)
                    eb_.slot[0].add_r(tok)
                    tau.slot[0].add_r(tok)
                    tok = ACT(tau2.t[:, ic:ic + 1], cnt.t[:, ic:ic + 1], AF.Exp,
                              scale=wcol.t[:, ic:ic + 1],
                              bias=biasall.t[:, ic:ic + 1],
                              waits=cnt.slot[ic].r() + wcol.slot[ic].r()
                              + biasall.slot[0].r() + tau2.slot[ic].w())
                    tau2.slot[ic].set_w(tok)
                    cnt.slot[ic].add_r(tok)
                    wcol.slot[ic].add_r(tok)
                    biasall.slot[0].add_r(tok)
                    tok = STT("vector", pb_.t[:], eb_.t[:], tau2.t[:, ic:ic + 1],
                              eb_.t[:], ALU.is_ge, ALU.mult,
                              accum=zacc.t[:, ic:ic + 1],
                              waits=eb_.slot[0].r() + tau2.slot[ic].r()
                              + pb_.slot[0].w() + zacc.slot[ic].w())
                    pb_.slot[0].set_w(tok)
                    zacc.slot[ic].set_w(tok)
                    eb_.slot[0].add_r(tok)
                    tau2.slot[ic].add_r(tok)
                    tok = RECIP(zinv.t[:, ic:ic + 1], zacc.t[:, ic:ic + 1],
                                waits=zacc.slot[ic].r() + zinv.slot[ic].w())
                    zinv.slot[ic].set_w(tok)
                    zacc.slot[ic].add_r(tok)
                    dg = diag[it % 2]
                    tok = TS("vector", dg.t[:], ident.t[:], zinv.t[:, ic:ic + 1],
                             op0=ALU.mult,
                             waits=ident.slot[0].r() + zinv.slot[ic].r()
                             + dg.slot[0].w())
                    dg.slot[0].set_w(tok)
                    ident.slot[0].add_r(tok)
                    zinv.slot[ic].add_r(tok)
                    if DEBUG and h == 1 and it == 0:
                        tk = b.dma("gpsimd", dbg["dbg_s"], eb_.t[:],
                                   waits=eb_.slot[0].r())
                        eb_.slot[0].add_r(tk)
                        tk = b.dma("gpsimd", dbg["dbg_p"], pb_.t[:],
                                   waits=pb_.slot[0].r())
                        pb_.slot[0].add_r(tk)

                def part2h(it, rnd):
                    pb_ = p_sb[it % 2]
                    wpos = it % 4
                    dg = diag[it % 2]
                    tp = tr_ps[rnd]
                    tvv = tr_v[rnd]
                    for j8 in range(8):
                        jc = rnd * 8 + j8
                        w1 = pb_.slot[0].r() + dg.slot[0].r()
                        if j8 == 0:
                            w1 = w1 + tp.slot[0].w()
                        tok = MM(tvv[:, j8, :], pb_.t[:, jc * P:(jc + 1) * P],
                                 dg.t[:], start=True, stop=True, waits=w1)
                        if j8 == 7:
                            tp.slot[0].set_w(tok)
                        pb_.slot[0].add_r(tok)
                        dg.slot[0].add_r(tok)
                    sidx = rnd * 4 + wpos
                    dstv = pT_v[:, rnd * 8:(rnd + 1) * 8,
                                wpos * P:(wpos + 1) * P]
                    tok = TS("vector", dstv, tvv[:, :, :], 0.0, op0=ALU.add,
                             waits=tp.slot[0].r() + pT.slot[sidx].w())
                    pT.slot[sidx].set_w(tok)
                    tp.slot[0].add_r(tok)

                def pv_block(it):
                    wnd = it // 4
                    for jc in range(16):
                        w1 = v_sb.slot[jc].r()
                        rg = jc // 8
                        for j in range(4):
                            w1 = w1 + pT.slot[rg * 4 + j].r()
                        if jc == 0:
                            w1 = w1 + ctx_ps.slot[0].w()
                        tok = MM(ctx_ps.t[:], v_sb.t[:, jc * P:(jc + 1) * P],
                                 pT.t[:, jc * 512:(jc + 1) * 512],
                                 start=(jc == 0), stop=(jc == 15), waits=w1)
                        v_sb.slot[jc].add_r(tok)
                        for s2 in range(8):
                            pT.slot[s2].add_r(tok)
                        if jc == 15:
                            ctx_ps.slot[0].set_w(tok)
                    tok = TS("vector", ctx_sb.t[:, wnd * 512:(wnd + 1) * 512],
                             ctx_ps.t[:], 0.0, op0=ALU.add,
                             waits=ctx_ps.slot[0].r() + ctx_sb.slot[wnd].w())
                    ctx_sb.slot[wnd].set_w(tok)
                    ctx_ps.slot[0].add_r(tok)

                sc_half(0, 0)
                exp_half(0, 0)
                sc_half(0, 1)
                exp_half(0, 1)
                rest2(0)
                sc_half(1, 0)
                exp_half(1, 0)
                sc_half(1, 1)
                for it in range(NIT):
                    if it + 1 < NIT:
                        exp_half(it + 1, 1)
                        rest2(it + 1)
                    if it + 2 < NIT:
                        sc_half(it + 2, 0)
                        exp_half(it + 2, 0)
                    part2h(it, 0)
                    if it + 2 < NIT:
                        sc_half(it + 2, 1)
                    part2h(it, 1)
                    if it % 4 == 3:
                        pv_block(it)

                wv = []
                for wnd in range(4):
                    wv += ctx_sb.slot[wnd].r()
                tok = b.dma("sync", ctx_scr[h * P:(h + 1) * P, :], ctx_sb.t[:],
                            waits=wv)
                for wnd in range(4):
                    ctx_sb.slot[wnd].add_r(tok)
                ctx_dma_toks[h] = tok
                if DEBUG and h == 1:
                    tk = b.dma("gpsimd", dbg["dbg_ctx"], ctx_sb.t[:], waits=wv)
                    for wnd in range(4):
                        ctx_sb.slot[wnd].add_r(tk)

        out_toks = []
        with ExitStack() as pd:
            out_ps = [ps(f"out_ps{i}", [P, T], F32, st=pd) for i in range(2)]
            ffn_t = [sb(f"ffn_t{i}", [P, T], F32, st=pd) for i in range(2)]
            x_t = [sb(f"x_t{i}", [P, T], F32, st=pd) for i in range(2)]
            ftsem = [dsem(f"fts{i}") for i in range(2)]
            xtsem = [dsem(f"xts{i}") for i in range(2)]
            stage = [sb(f"stage{i}", [P, T], F32, st=pd) for i in range(2)]

            tok = b.dma("scalar", xn.t[:, NKC - 1, :],
                        ctx_scr[(NKC - 1) * P:NKC * P, :],
                        waits=xn.slot[NKC - 1].w() + [ctx_dma_toks[NKC - 1]],
                        sctr=ctxsem[0])
            xn.slot[NKC - 1].set_w(tok)

            wo_ring = wring["q"] + wring["k"] + wring["v"]
            NR = len(wo_ring)
            wo_sems = wsem["q"] + wsem["k"] + wsem["v"]
            wo_toks = {}

            def prefetch_ot(ot):
                for g in range(8):
                    idx = ot * 8 + g
                    sl = wo_ring[idx % NR]
                    base = (ot * 8 + g) * P
                    view = wT["o"][base:base + P, :].rearrange(
                        "p (j c) -> p j c", c=P)
                    tok = b.dma("gpsimd", sl.t[:], view, waits=sl.slot[0].w(),
                                sctr=wo_sems[idx % NR])
                    sl.slot[0].set_w(tok)
                    wo_toks[idx] = tok

            prefetch_ot(0)
            prefetch_ot(1)
            tok = b.dma("scalar", ffn_t[0].t[:], ffn_scr[0:P, :],
                        waits=ffn_t[0].slot[0].w() + ffn_out_toks[0],
                        sctr=ftsem[0])
            ffn_t[0].slot[0].set_w(tok)
            tok = b.dma("scalar", x_t[0].t[:], x_fm[0:P, :],
                        waits=x_t[0].slot[0].w(), sctr=xtsem[0])
            x_t[0].slot[0].set_w(tok)
            for ot in range(NKC):
                if ot + 2 < NKC:
                    prefetch_ot(ot + 2)
                if ot + 1 < NKC:
                    ftn = ffn_t[(ot + 1) % 2]
                    tok = b.dma("scalar", ftn.t[:],
                                ffn_scr[(ot + 1) * P:(ot + 2) * P, :],
                                waits=ftn.slot[0].w() + ffn_out_toks[ot + 1],
                                sctr=ftsem[(ot + 1) % 2])
                    ftn.slot[0].set_w(tok)
                    xtn = x_t[(ot + 1) % 2]
                    tok = b.dma("scalar", xtn.t[:],
                                x_fm[(ot + 1) * P:(ot + 2) * P, :],
                                waits=xtn.slot[0].w(), sctr=xtsem[(ot + 1) % 2])
                    xtn.slot[0].set_w(tok)

                ops_ = out_ps[ot % 2]
                for hc in range(NKC):
                    idx = ot * 8 + hc // 4
                    sl = wo_ring[idx % NR]
                    lhs = sl.t[:, hc % 4, :]
                    for n in range(4):
                        w1 = [wo_toks[idx]] + xn.slot[hc].r()
                        if hc == 0 and n == 0:
                            w1 = w1 + ops_.slot[0].w()
                        tok = MM(ops_.t[:, n * 512:(n + 1) * 512], lhs,
                                 xn.t[:, hc, n * 512:(n + 1) * 512],
                                 start=(hc == 0), stop=(hc == NKC - 1), waits=w1)
                        sl.slot[0].add_r(tok)
                        xn.slot[hc].add_r(tok)
                        if hc == NKC - 1 and n == 3:
                            ops_.slot[0].set_w(tok)

                st_ = stage[ot % 2]
                ft = ffn_t[ot % 2]
                for hf in range(2):
                    c0, c1 = hf * 1024, (hf + 1) * 1024
                    tok = TS("vector", st_.t[:, c0:c1], ops_.t[:, c0:c1],
                             fbh.t[:, ot:ot + 1], op0=ALU.add,
                             waits=ops_.slot[0].r() + fbh.slot[0].r()
                             + st_.slot[0].w())
                    st_.slot[0].set_w(tok)
                    ops_.slot[0].add_r(tok)
                    fbh.slot[0].add_r(tok)
                tok = TT("vector", st_.t[:], st_.t[:], ft.t[:], ALU.add,
                         waits=st_.slot[0].r() + ft.slot[0].r())
                st_.slot[0].set_w(tok)
                ft.slot[0].add_r(tok)
                xt = x_t[ot % 2]
                tok = TT("gpsimd", st_.t[:], st_.t[:], xt.t[:], ALU.add,
                         waits=st_.slot[0].r() + xt.slot[0].r())
                st_.slot[0].set_w(tok)
                xt.slot[0].add_r(tok)
                tok = b.dma(("sync", "gpsimd")[ot % 2],
                            out_fm[ot * P:(ot + 1) * P, :], st_.t[:],
                            waits=st_.slot[0].r())
                st_.slot[0].add_r(tok)
                out_toks.append(tok)

            b.op("sync", lambda E: E.nop(), waits=out_toks)

        with nc.Block() as blk:
            b.emit(blk)

        counts = {e: b.ctr[e].n for e in ENGS}
        counts.update({f"dma_{e}": c.n // 16 for e, c in b.dctr.items()})
        print("kernel instruction counts:", counts)

    return nc


_NC_CACHE = {}


def get_nc():
    key = DEBUG
    if key not in _NC_CACHE:
        _NC_CACHE[key] = build_nc()
    return _NC_CACHE[key]


def retile_w(w):
    wt = np.asarray(w, np.float32).T
    t = wt.reshape(8, 4, 128, 32, 128).transpose(3, 0, 2, 1, 4)
    return np.ascontiguousarray(t.reshape(32 * 8 * 128, 512))


def make_in_maps(inputs):
    x = np.ascontiguousarray(np.asarray(inputs["L0_attn_hidden_states"],
                                        np.float32))
    wq = retile_w(inputs["L0_attn_wq"])
    wk = retile_w(inputs["L0_attn_wk"])
    wv = retile_w(inputs["L0_attn_wv"])
    wo = retile_w(inputs["L0_attn_wo"])
    shared = {
        "wqT": wq, "wkT": wk, "wvT": wv, "woT": wo,
        "bq": np.asarray(inputs["L0_attn_bq"], np.float32),
        "bk": np.asarray(inputs["L0_attn_bk"], np.float32),
        "bv": np.asarray(inputs["L0_attn_bv"], np.float32),
        "bo": np.asarray(inputs["L0_attn_bo"], np.float32),
        "qw": np.asarray(inputs["L0_ffn_qweight"], np.int32),
        "qz": np.asarray(inputs["L0_ffn_qzeros"], np.int32),
        "sc": np.asarray(inputs["L0_ffn_scales"], np.float32),
        "fb": np.asarray(inputs["L0_ffn_bias"], np.float32),
    }
    maps = []
    for bidx in range(x.shape[0]):
        m = dict(shared)
        m["x_fm"] = np.ascontiguousarray(x[bidx].T)
        maps.append(m)
    return maps


def kernel(**inputs):
    nc = get_nc()
    in_maps = make_in_maps(inputs)
    res = run_bass_kernel_spmd(nc, in_maps, core_ids=list(range(len(in_maps))))
    out = np.stack([np.asarray(r["out_fm"]).T for r in res.results])
    return np.ascontiguousarray(out).astype(np.float32)
